# version 35
# baseline (speedup 1.0000x reference)
import numpy as np
import ml_dtypes

import concourse.bass as bass
import concourse.mybir as mybir
from concourse.bass_utils import run_bass_kernel_spmd

F32 = mybir.dt.float32
BF16 = mybir.dt.bfloat16
I32 = mybir.dt.int32
AF = mybir.ActivationFunctionType
ALU = mybir.AluOpType
AX = mybir.AxisListType

BF16NP = ml_dtypes.bfloat16

B, T, D = 4, 4096, 1024
H = 16
HID = 3072
NOWN = 2048
SCALE = 0.125
EXPC = 12.0
EPS = 1e-5
Q127 = 1.0 / 127.0

NT_F = T // 128
NT_O = NOWN // 128
GSZ = 2
NG_F = NT_F // GSZ
NG_O = NT_O // GSZ
VS = 66
VSP = 132

NYS = 12
NGS = 4

CONST_VALS = [-EXPC, EPS, 1.0 / D, Q127, 8.0, SCALE, 1.0, 0.5]

ARM_WINDOWS = {0: [(0, 0), (3072, 1024)], 1: [(1024, 0), (2048, 1024)]}
ARM_W512 = {
    0: [(0, 0), (512, 512), (3072, 1024), (3584, 1536)],
    1: [(1024, 0), (1536, 512), (2048, 1024), (2560, 1536)],
}

ENGS = ("t", "s", "v", "y", "g")


class SwitchItem:
    def __init__(self):
        self.arms = {0: {e: [] for e in ENGS}, 1: {e: [] for e in ENGS}}


SEM_OWNER = {"ct": "t", "cs": "s", "cv": "v"}


class PB:
    def __init__(self):
        self.items = {e: [] for e in ENGS}
        self.cnt = {}
        self.waited = {e: {} for e in ENGS}
        self._arm = None
        self._switch_item = None
        self._arm_snapshot = None
        self._arm_counts = None
        self._arm_waited = None
        self._arm_waited_snap = None

    def emit(self, e, f):
        if self._arm is not None:
            si, a = self._arm
            si.arms[a][e].append(f)
        else:
            self.items[e].append(f)

    def filter_waits(self, e, waits):
        kept = []
        w = self.waited[e]
        for s, v in waits:
            if not v:
                continue
            if v <= w.get(s, 0):
                continue
            w[s] = v
            kept.append((s, v))
        return kept

    def inc(self, sem, by=1):
        self.cnt[sem] = self.cnt.get(sem, 0) + by
        return self.cnt[sem]

    def get(self, sem):
        return self.cnt.get(sem, 0)

    def begin_switch(self):
        si = SwitchItem()
        for e in ENGS:
            self.items[e].append(si)
        self._switch_item = si
        self._arm_snapshot = dict(self.cnt)
        self._arm_waited_snap = {e: dict(self.waited[e]) for e in ENGS}
        self._arm_counts = []
        self._arm_waited = []

    def begin_arm(self, a):
        self.cnt = dict(self._arm_snapshot)
        self.waited = {e: dict(self._arm_waited_snap[e]) for e in ENGS}
        self._arm = (self._switch_item, a)

    def end_arm(self):
        self._arm_counts.append(dict(self.cnt))
        self._arm_waited.append(self.waited)
        self._arm = None

    def end_switch(self):
        c0, c1 = self._arm_counts
        assert c0 == c1, f"arm sem counts differ:\n{c0}\n{c1}"
        self.cnt = c0
        w0, w1 = self._arm_waited
        self.waited = {
            e: {s: min(w0[e][s], w1[e][s]) for s in w0[e] if s in w1[e]}
            for e in ENGS}
        self._switch_item = None


def build_program():
    nc = bass.Bass("TRN2", target_bir_lowering=False)

    xf_h = nc.dram_tensor("xf", [T, D], F32, kind="ExternalInput").ap()
    xo_h = nc.dram_tensor("xo", [NOWN, D], F32, kind="ExternalInput").ap()
    wqkv_h = nc.dram_tensor("wqkv", [D, 3 * D], F32, kind="ExternalInput").ap()
    wout_h = nc.dram_tensor("wout", [D, D], F32, kind="ExternalInput").ap()
    dgeo_h = nc.dram_tensor("dgeo", [128, 16], F32, kind="ExternalInput").ap()
    hsq_h = nc.dram_tensor("hsq", [16, 128], F32, kind="ExternalInput").ap()
    dq_h = nc.dram_tensor("dq", [16, 128], F32, kind="ExternalInput").ap()
    wgu_h = nc.dram_tensor("wgu", [24, D, 256], I32, kind="ExternalInput").ap()
    wdt_h = nc.dram_tensor("wdt", [HID, D], I32, kind="ExternalInput").ap()
    cst_h = nc.dram_tensor("cst", [128, len(CONST_VALS)], F32, kind="ExternalInput").ap()
    stair_h = nc.dram_tensor("stair", [128, 128], BF16, kind="ExternalInput").ap()
    ident_h = nc.dram_tensor("ident", [128, 128], BF16, kind="ExternalInput").ap()
    y_h = nc.dram_tensor("y", [NOWN, D], F32, kind="ExternalOutput").ap()

    k_scr = nc.dram_tensor("k_scr", [8, 128, T], BF16, kind="Internal").ap()
    v_scr = nc.dram_tensor("v_scr", [8, NT_F, 128, 128], BF16, kind="Internal").ap()
    av_scr = nc.dram_tensor("av_scr", [8, 128, NOWN], BF16, kind="Internal").ap()
    xn_scr = nc.dram_tensor("xn_scr", [NT_O, 128, D], F32, kind="Internal").ap()
    q_scr = nc.dram_tensor("q_scr", [8, 128, NOWN], BF16, kind="Internal").ap()

    pb = PB()

    SEM = {}
    for name in (["ct", "cs", "cv", "bar"] + [f"dy{i}" for i in range(NYS)]
                 + [f"dg{i}" for i in range(NGS)]):
        SEM[name] = nc.semaphore(name).__enter__()
    CT, CS, CV, BAR = SEM["ct"], SEM["cs"], SEM["cv"], SEM["bar"]

    alive = []

    def sbuf(name, shape, dt):
        gd = nc.sbuf_tensor(name, shape, dt)
        t = gd.__enter__()
        alive.append((name, gd))
        return t

    def sb_free(names):
        nset = set(names)
        while nset:
            name, gd = alive[-1]
            assert name in nset, f"non-LIFO free: top={name}, want={nset}"
            gd.__exit__(None, None, None)
            alive.pop()
            nset.remove(name)

    cst_sb = sbuf("cst_sb", [128, len(CONST_VALS)], F32)
    stair_sb = sbuf("stair_sb", [128, 128], BF16)
    ident_sb = sbuf("ident_sb", [128, 128], BF16)
    dgeo_sb = sbuf("dgeo_sb", [128, 16], F32)
    dgeo_bf = sbuf("dgeo_bf", [128, 16], BF16)
    hsq_sb = sbuf("hsq_sb", [16, 128], F32)
    dq_sb = sbuf("dq_sb", [16, 128], F32)
    sdm = sbuf("sdm", [16, 128], BF16)

    for i, val in enumerate(CONST_VALS):
        nc.const_aps.register(F32, val, cst_sb[:, i:i + 1])

    ps_guard = nc.psum_tensor("ps_all", [128, 4096], F32)
    PS = ps_guard.__enter__()

    def bank(b0, nb=1):
        return PS[:, 512 * b0: 512 * (b0 + nb)]

    def emitw(e, waits, f):
        wl = tuple((SEM[s], v) for (s, v) in pb.filter_waits(e, waits))
        def g(eng, wl=wl, f=f):
            for s, v in wl:
                eng.wait_ge(s, v)
            f(eng)
        pb.emit(e, g)

    def dma(out_ap, in_ap, transpose=False, waits=()):
        i = pb.get("_yrr") % NYS
        pb.inc("_yrr")
        name = f"dy{i}"
        pre = pb.get(name)
        post = pb.inc(name)
        wl = [(name, 16 * pre)] if pre else []
        wl += [w for w in waits if w and w[1]]
        wlh = tuple((SEM[s], v) for (s, v) in pb.filter_waits("y", wl))
        def f(eng, o=out_ap, i_=in_ap, tr=transpose, wlh=wlh, sem=SEM[name]):
            for s, v in wlh:
                eng.wait_ge(s, v)
            eng.dma_start(out=o, in_=i_, transpose=tr).then_inc(sem, 16)
        pb.emit("y", f)
        return (name, 16 * post)

    def gdma(out_ap, in_ap, waits=()):
        i = pb.get("_grr") % NGS
        pb.inc("_grr")
        name = f"dg{i}"
        pre = pb.get(name)
        post = pb.inc(name)
        wl = [(name, 16 * pre)] if pre else []
        wl += [w for w in waits if w and w[1]]
        wlh = tuple((SEM[s], v) for (s, v) in pb.filter_waits("g", wl))
        def f(eng, o=out_ap, i_=in_ap, wlh=wlh, sem=SEM[name]):
            for s, v in wlh:
                eng.wait_ge(s, v)
            eng.dma_start(out=o, in_=i_).then_inc(sem, 16)
        pb.emit("g", f)
        return (name, 16 * post)

    def comp(tokens):
        d = {}
        for tk in tokens:
            if not tk:
                continue
            n, v = tk
            d[n] = max(d.get(n, 0), v)
        return [(n, v) for n, v in d.items()]

    def barrier():
        ywaits = [(f"dy{i}", 16 * pb.get(f"dy{i}")) for i in range(NYS)]
        gwaits = [(f"dg{i}", 16 * pb.get(f"dg{i}")) for i in range(NGS)]
        own = {"t": [("ct", pb.get("ct"))], "s": [("cs", pb.get("cs"))],
               "v": [("cv", pb.get("cv"))], "y": ywaits, "g": gwaits}
        k = pb.inc("bar", 5) // 5
        for e in ENGS:
            def f(eng, kk=k):
                eng.nop().then_inc(BAR, 1)
                eng.wait_ge(BAR, 5 * kk)
            emitw(e, own[e], f)
        allw = [w for lst in own.values() for w in lst]
        for e in ENGS:
            for s, v in allw:
                if v > pb.waited[e].get(s, 0):
                    pb.waited[e][s] = v

    t_cst = dma(cst_sb[:], cst_h[:])
    dma(stair_sb[:], stair_h[:])
    t_ident = dma(ident_sb[:], ident_h[:])
    t_dgeo = dma(dgeo_sb[:], dgeo_h[:])
    t_hsq = dma(hsq_sb[:], hsq_h[:])
    t_dq = dma(dq_sb[:], dq_h[:])

    def f_sd(e):
        e.scalar_tensor_tensor(sdm[:], dq_sb[:], 8.0, hsq_sb[:],
                               op0=ALU.mult, op1=ALU.mult).then_inc(CV, 1)
        e.tensor_copy(dgeo_bf[:], dgeo_sb[:]).then_inc(CV, 1)
    emitw("v", [t_cst, t_dgeo, t_hsq, t_dq], f_sd)
    pb.inc("cv", 2)
    c_sd = pb.get("cv")

    Vsb0 = sbuf("Vsb0", [128, NT_F * VSP], BF16)

    NLS = 3
    xg = [sbuf(f"xg{s}", [128, GSZ * D], F32) for s in range(NLS)]
    htg = [sbuf(f"htg{s}", [128, GSZ * D], BF16) for s in range(NLS)]
    hfg = [sbuf(f"hfg{s}", [128, GSZ * D], BF16) for s in range(NLS)]
    sqd = sbuf("sqd", [128, D], F32)
    statb = sbuf("statb", [128, 3 * 8], F32)
    Kst = [sbuf(f"Kst{s}", [128, 2048], BF16) for s in range(2)]
    Vst = [sbuf(f"Vst{s}", [128, 1024], BF16) for s in range(2)]
    wkv = sbuf("wkv", [128, 8 * 2048], BF16)

    t_wkv = comp([gdma(wkv[:, 2048 * k: 2048 * (k + 1)],
                       wqkv_h[128 * k: 128 * (k + 1), D: 3 * D])
                  for k in range(8)])

    hz = {
        "stat_rd": [0] * NLS, "sq_wr": 0,
        "xg_rd_v": [0] * NLS, "xg_rd_s": [0] * NLS,
        "ht_tr": [[] for _ in range(NLS)], "hf_rd": [0] * NLS,
    }

    def ln_group(slot, t_loads):
        st = statb[:, 8 * slot: 8 * slot + 8]
        c_ht = 0
        c_hts = []
        for i in range(GSZ):
            t_load = t_loads[i]
            xt = xg[slot][:, D * i: D * (i + 1)]
            ht = htg[slot][:, D * i: D * (i + 1)]
            emitw("v", [t_load, ("cv", hz["stat_rd"][slot])],
                  lambda e, xt=xt, st=st: e.reduce_sum(
                      st[:, 0:1], xt, axis=AX.X).then_inc(CV, 1))
            c_sum = pb.inc("cv")
            emitw("s", [t_load, ("cs", hz["sq_wr"])],
                  lambda e, xt=xt, st=st: e.activation(
                      sqd[:], xt, AF.Square, accum_out=st[:, 1:2]).then_inc(CS, 1))
            c_sq = pb.inc("cs")
            hz["sq_wr"] = c_sq

            def f_stats(e, st=st, ws=c_sum, wq=c_sq):
                e.wait_ge(CV, ws)
                e.wait_ge(CS, wq)
                e.tensor_scalar(st[:, 2:3], st[:, 0:1], 1.0 / D, None,
                                op0=ALU.mult).then_inc(CV, 1)
                e.wait_ge(CV, ws + 1)
                e.tensor_mul(st[:, 3:4], st[:, 2:3], st[:, 2:3]).then_inc(CV, 1)
                e.wait_ge(CV, ws + 2)
                e.scalar_tensor_tensor(st[:, 4:5], st[:, 1:2], 1.0 / D, st[:, 3:4],
                                       op0=ALU.mult, op1=ALU.subtract).then_inc(CV, 1)
            pb.emit("v", f_stats)
            pb.inc("cv", 2)
            c_var = pb.inc("cv")
            emitw("s", [("cv", c_var)],
                  lambda e, st=st: e.activation(st[:, 5:6], st[:, 4:5], AF.Sqrt,
                                                bias=EPS).then_inc(CS, 1))
            c_std = pb.inc("cs")
            emitw("v", [("cs", c_std)],
                  lambda e, st=st: e.reciprocal(st[:, 6:7], st[:, 5:6]).then_inc(CV, 1))
            c_rstd = pb.inc("cv")
            wl = [("cv", c_rstd)] + (hz["ht_tr"][slot] or [])
            emitw("v", wl,
                  lambda e, st=st, xt=xt, ht=ht: e.tensor_scalar(
                      ht, xt, st[:, 2:3], st[:, 6:7],
                      op0=ALU.subtract, op1=ALU.mult).then_inc(CV, 1))
            c_ht = pb.inc("cv")
            c_hts.append(c_ht)
        hz["stat_rd"][slot] = c_ht
        hz["xg_rd_v"][slot] = c_ht
        hz["xg_rd_s"][slot] = hz["sq_wr"]
        return c_hts

    tr_state = {"evac": [0] * GSZ}

    def transpose_group(slot, c_hts, ps_base):
        c_tr = 0
        for i in range(GSZ):
            ptr = PS[:].bitcast(BF16)[:, 1024 * (ps_base + i): 1024 * (ps_base + i + 1)]
            wl = [t_ident, ("cv", c_hts[i])]
            if tr_state["evac"][i]:
                wl.append(("cs", tr_state["evac"][i]))

            def f_tr(e, slot=slot, i=i, ptr=ptr):
                ins = None
                for k in range(8):
                    ins = e.transpose(
                        ptr[:, 128 * k: 128 * (k + 1)],
                        htg[slot][:, 1024 * i + 128 * k: 1024 * i + 128 * (k + 1)],
                        ident_sb[:])
                ins.then_inc(CT, 1)
            emitw("t", wl, f_tr)
            c_tr = pb.inc("ct")

            def f_trev(e, slot=slot, i=i, ptr=ptr):
                out3 = hfg[slot][:].rearrange("p (k w) -> p k w", k=8)[:, :, 128 * i: 128 * (i + 1)]
                in3 = ptr.rearrange("p (k w) -> p k w", k=8)
                e.copy(out3, in3).then_inc(CS, 1)
            emitw("s", [("ct", c_tr), ("ct", hz["hf_rd"][slot])], f_trev)
            tr_state["evac"][i] = pb.inc("cs")
        hz["ht_tr"][slot] = [("ct", c_tr)]
        return [("cs", tr_state["evac"][GSZ - 1])]

    psK = bank(0, 4)
    psV = bank(4, 2)
    c_kv_evac = 0
    c_v_evac = 0
    t_kst = [None, None]
    t_vst = [None, None]

    ld_toks = {}
    tr_toks = {}

    def p1_load(g):
        slot = g % NLS
        toks = []
        for i in range(GSZ):
            t0 = GSZ * g + i
            w = [("cv", hz["xg_rd_v"][slot]), ("cs", hz["xg_rd_s"][slot])] if i == 0 else ()
            toks.append(dma(xg[slot][:, D * i: D * (i + 1)],
                            xf_h[128 * t0: 128 * (t0 + 1), :], waits=w))
        ld_toks[g] = toks

    def p1_ln(g):
        slot = g % NLS
        c_hts = ln_group(slot, ld_toks[g])
        hz["xg_rd_s"][slot] = hz["sq_wr"]
        return c_hts

    state = {"c_kv_evac": 0, "c_v_evac": 0,
             "t_kst": [None, None], "t_vst": [None, None]}
    ks_toks = []
    vs_toks = []

    def p1_kmm_ev(g):
        slot = g % NLS
        waits = tr_toks[g] + t_wkv + (
            [("cs", state["c_kv_evac"])] if state["c_kv_evac"] else [])

        def k_mms(e, slot=slot):
            ins = None
            for k in range(8):
                for m in range(8):
                    ins = e.matmul(psK[:, 256 * m: 256 * (m + 1)],
                                   wkv[:, 2048 * k + 128 * m: 2048 * k + 128 * (m + 1)],
                                   hfg[slot][:, 256 * k: 256 * (k + 1)],
                                   start=(k == 0 and m % 2 == 0),
                                   stop=(k == 7 and m % 2 == 1))
            ins.then_inc(CT, 1)
        emitw("t", waits, k_mms)
        c_kmm = pb.inc("ct")

        kslot = g % 2
        wl = [("ct", c_kmm)] + ([state["t_kst"][kslot]] if state["t_kst"][kslot] else [])
        emitw("s", wl, lambda e, kslot=kslot: e.copy(
            Kst[kslot][:], PS[:, 0:2048]).then_inc(CS, 1))
        state["c_kv_evac"] = pb.inc("cs")
        state["t_kst"][kslot] = dma(
            bass.AP(k_scr.tensor, 256 * g, [[T, 128], [128 * T, 8], [1, 256]]),
            bass.AP(Kst[kslot], 0, [[2048, 128], [256, 8], [1, 256]]),
            waits=[("cs", state["c_kv_evac"])])
        ks_toks.append(state["t_kst"][kslot])

    def p1_vmm_ev(g):
        slot = g % NLS
        c_vmm = []
        for i in range(GSZ):
            wlv = [("cs", state["c_v_evac"])] if state["c_v_evac"] else []

            def v_mms(e, slot=slot, i=i):
                ins = None
                for half in range(2):
                    for k in range(8):
                        ins = e.matmul(
                            psV[:, 512 * half: 512 * (half + 1)],
                            hfg[slot][:, 256 * k + 128 * i: 256 * k + 128 * (i + 1)],
                            wkv[:, 2048 * k + 1024 + 512 * half:
                                2048 * k + 1024 + 512 * (half + 1)],
                            start=(k == 0), stop=(k == 7))
                ins.then_inc(CT, 1)
            emitw("t", wlv, v_mms)
            c_vmm.append(pb.inc("ct"))
            t0 = GSZ * g + i
            vslot = t0 % 2
            wl = [("ct", c_vmm[i])]
            if state["c_v_evac"]:
                wl.append(("cs", state["c_v_evac"]))
            if state["t_vst"][vslot]:
                wl.append(state["t_vst"][vslot])
            emitw("s", wl, lambda e, vslot=vslot: e.copy(
                Vst[vslot][:], PS[:, 2048:3072]).then_inc(CS, 1))
            state["c_v_evac"] = pb.inc("cs")
            state["t_vst"][vslot] = dma(
                bass.AP(v_scr.tensor, 128 * 128 * t0,
                        [[128, 128], [NT_F * 128 * 128, 8], [1, 128]]),
                bass.AP(Vst[vslot], 0, [[1024, 128], [128, 8], [1, 128]]),
                waits=[("cs", state["c_v_evac"])])
            vs_toks.append(state["t_vst"][vslot])
        hz["hf_rd"][slot] = c_vmm[-1]

    hts = {}
    p1_load(0)
    p1_load(1)
    p1_load(2)
    hts[0] = p1_ln(0)
    tr_toks[0] = transpose_group(0 % NLS, hts[0], 6)
    hts[1] = p1_ln(1)
    for g in range(NG_F):
        if g + 3 < NG_F:
            p1_load(g + 3)
        if g + 2 < NG_F:
            hts[g + 2] = p1_ln(g + 2)
        if g + 1 < NG_F:
            tr_toks[g + 1] = transpose_group((g + 1) % NLS, hts[g + 1], 6)
        p1_kmm_ev(g)
        p1_vmm_ev(g)

    c_p1_last_mm = hz["hf_rd"][(NG_F - 1) % NLS]
    sb_free({"wkv"})



    Qt = sbuf("Qt", [128, 8 * NOWN], BF16)
    wq = sbuf("wq", [128, 8 * 1024], BF16)
    t_wq = comp([gdma(wq[:, 1024 * k: 1024 * (k + 1)],
                      wqkv_h[128 * k: 128 * (k + 1), 0:D]) for k in range(8)])

    t_v0 = comp([gdma(
        bass.AP(Vsb0, 66 * hp,
                [[NT_F * VSP, 128], [VSP, NT_F], [1, 64]]),
        bass.AP(v_scr.tensor, 64 * hp,
                [[128, 128], [128 * 128, NT_F], [1, 64]]),
        waits=comp(vs_toks)) for hp in range(2)])
    w8t = sbuf("w8t", [128, 8 * 128], BF16)
    u_sb = sbuf("u_sb", [16, 8 * 128], BF16)
    wstg = [sbuf(f"wstg{s}", [128, 128], BF16) for s in range(2)]

    ps_trf = [PS[:].bitcast(BF16)[:, 0:128],
              PS[:].bitcast(BF16)[:, 3 * 1024: 3 * 1024 + 128]]
    ps_uf = PS[0:16, 512:640]
    ps_ff = PS[:, 1024:1152]
    c_w8h = {}
    c_trfh = {}
    c_ue = 0
    c_fadd = 0
    for k in range(8):
        s2 = k % 2
        wl = t_wq + [("cv", c_sd)]
        if k >= 2:
            wl.append(("ct", c_trfh[k - 2]))

        def f_stg(e, k=k, s2=s2):
            for j in range(2):
                out3 = wstg[s2][:, 64 * j: 64 * (j + 1)].rearrange(
                    "p (m a) -> p m a", a=8)
                in3 = bass.AP(wq, 1024 * k + 64 * j,
                              [[8192, 128], [128, 8], [1, 8]])
                ins = e.tensor_copy(out3, in3)
            ins.then_inc(CV, 1)
        emitw("v", wl, f_stg)
        c_stg = pb.inc("cv")

        wl = [t_ident, ("cs", state["c_kv_evac"]), ("cv", c_stg)]
        if k >= 2:
            wl.append(("cs", c_w8h[k - 2]))
        if c_fadd:
            wl.append(("cv", c_fadd))

        def f_trf(e, k=k, s2=s2):
            e.transpose(ps_trf[s2][:], wstg[s2][:], ident_sb[:]).then_inc(CT, 1)
        emitw("t", wl, f_trf)
        c_trfh[k] = pb.inc("ct")

        def f_w8e(e, k=k, s2=s2):
            e.copy(w8t[:, 128 * k: 128 * (k + 1)], ps_trf[s2][:]).then_inc(CS, 1)
        emitw("s", [("ct", c_trfh[k])], f_w8e)
        c_w8h[k] = pb.inc("cs")
        c_w8 = c_w8h[k]

        wl = [("cs", c_w8)]
        if c_ue:
            wl.append(("cs", c_ue))

        def f_umm(e, k=k):
            e.matmul(ps_uf, dgeo_bf[:],
                     w8t[:, 128 * k: 128 * (k + 1)],
                     start=True, stop=True).then_inc(CT, 1)
        emitw("t", wl, f_umm)
        c_umm = pb.inc("ct")

        def f_ue(e, k=k):
            e.copy(u_sb[:, 128 * k: 128 * (k + 1)], ps_uf).then_inc(CS, 1)
        emitw("s", [("ct", c_umm)], f_ue)
        c_ue = pb.inc("cs")

        wl = [("cs", c_ue)]
        if c_fadd:
            wl.append(("cv", c_fadd))

        def f_fmm(e, k=k):
            e.matmul(ps_ff, u_sb[0:16, 128 * k: 128 * (k + 1)], sdm[:],
                     start=True, stop=True).then_inc(CT, 1)
        emitw("t", wl, f_fmm)
        c_fmm = pb.inc("ct")

        for j in range(2):
            def f_fadd(e, k=k, j=j):
                wcols = bass.AP(wq, 1024 * k + 64 * j,
                                [[8192, 128], [128, 8], [1, 8]])
                pcols = bass.AP(PS, 1024 + 8 * j,
                                [[4096, 128], [16, 8], [1, 8]])
                e.tensor_add(wcols, wcols, pcols).then_inc(CV, 1)
            emitw("v", [("ct", c_fmm)], f_fadd)
            c_fadd = pb.inc("cv")
    c_fold = c_fadd

    psQ = bank(0, 4)
    q_state = {"c_q_evac": [state["c_kv_evac"], state["c_kv_evac"]]}
    tr_state["evac"] = [state["c_v_evac"]] * GSZ

    def p2_load(g):
        slot = g % NLS
        toks = []
        for i in range(GSZ):
            t0 = GSZ * g + i
            w = [("cv", hz["xg_rd_v"][slot]), ("cs", hz["xg_rd_s"][slot])] if i == 0 else ()
            toks.append(dma(xg[slot][:, D * i: D * (i + 1)],
                            xo_h[128 * t0: 128 * (t0 + 1), :], waits=w))
        ld_toks[g] = toks

    def p2_mm_ev(g):
        slot = g % NLS
        for h in range(2):
            waits = tr_toks[g] + t_wq + [("cv", c_fold)] + (
                [("cs", q_state["c_q_evac"][h])] if q_state["c_q_evac"][h] else [])

            def q_mms(e, slot=slot, h=h):
                ins = None
                for k in range(8):
                    for m in range(4 * h, 4 * h + 4):
                        ins = e.matmul(psQ[:, 256 * m: 256 * (m + 1)],
                                       wq[:, 1024 * k + 128 * m: 1024 * k + 128 * (m + 1)],
                                       hfg[slot][:, 256 * k: 256 * (k + 1)],
                                       start=(k == 0 and m % 2 == 0),
                                       stop=(k == 7 and m % 2 == 1))
                ins.then_inc(CT, 1)
            emitw("t", waits, q_mms)
            c_qmm = pb.inc("ct")
            if h == 1:
                hz["hf_rd"][slot] = c_qmm

            def q_evac(e, g=g, h=h):
                out3 = Qt[:].rearrange("p (m w) -> p m w", m=8)[:, 4 * h: 4 * h + 4,
                                                               256 * g: 256 * (g + 1)]
                in3 = PS[:, 1024 * h: 1024 * (h + 1)].rearrange("p (m w) -> p m w", m=4)
                e.activation(out3, in3, AF.Copy, scale=SCALE).then_inc(CS, 1)
            emitw("s", [("ct", c_qmm)], q_evac)
            q_state["c_q_evac"][h] = pb.inc("cs")

    def p2_ln(g):
        slot = g % NLS
        c = ln_group(slot, ld_toks[g])
        hz["xg_rd_s"][slot] = hz["sq_wr"]
        return c

    hts = {}
    p2_load(0)
    p2_load(1)
    p2_load(2)
    hts[0] = p2_ln(0)
    tr_toks[0] = transpose_group(0 % NLS, hts[0], 4)
    hts[1] = p2_ln(1)
    for g in range(NG_O):
        if g + 3 < NG_O:
            p2_load(g + 3)
        if g + 2 < NG_O:
            hts[g + 2] = p2_ln(g + 2)
        if g + 1 < NG_O:
            tr_toks[g + 1] = transpose_group((g + 1) % NLS, hts[g + 1], 4)
        p2_mm_ev(g)

    barrier()

    c_qe_fin = max(q_state["c_q_evac"])
    c_qmm_fin = pb.get("ct")
    t_qsp = {}
    for c in range(8):
        du = dma if c < 2 else gdma
        t_qsp[c] = du(q_scr[c], Qt[:, 2048 * c: 2048 * (c + 1)],
                      waits=[("cs", c_qe_fin)])

    c_geo_t = c_qmm_fin
    c_geo_qpe = c_qe_fin
    c_r1_add = 0
    sb_free({"wstg0", "wstg1", "u_sb", "w8t", "wq", "Qt", "Vst0", "Vst1", "Kst0", "Kst1", "statb",
             "sqd", "hfg0", "hfg1", "hfg2", "htg0", "htg1", "htg2",
             "xg0", "xg1", "xg2"})

    h2 = sbuf("h2", [128, 8 * NOWN], BF16)
    wdT = sbuf("wdT", [128, 24 * 1024], BF16)
    NWS = 2
    wguT = [sbuf(f"wguT{s}", [128, 8 * 256], BF16) for s in range(NWS)]

    wout_bf = sbuf("wout_bf", [128, 8 * 1024], BF16)

    Ksb = [sbuf(f"Ksb{s}", [128, T], BF16) for s in range(2)]
    Qsb = [sbuf(f"Qsb{s}", [128, NOWN], BF16) for s in range(2)]

    Vsb = [Vsb0, sbuf("Vsb1", [128, NT_F * VSP], BF16)]
    Pt = [sbuf(f"Pt{s}", [128, 1024], BF16) for s in range(3)]
    AVpair = [sbuf(f"AVpair{s}", [128, NOWN], BF16) for s in range(2)]
    AVst = [sbuf(f"AVst{s}", [128, NOWN], BF16) for s in range(2)]
    recip_sb = sbuf("recip_sb", [128, 16], F32)

    psS = [bank(0, 2), bank(2, 2)]
    psAV = [bank(4, 2), bank(6, 2)]

    def prefetch_pair(c, waits_kv):
        wk = list(waits_kv) + (comp(ks_toks) if c < 2 else [])
        wv = list(waits_kv) + (comp(vs_toks) if c < 2 else [])
        dk0 = dma(Ksb[c % 2][:], k_scr[c], waits=wk)
        dq = dma(Qsb[c % 2][:], q_scr[c], waits=list(waits_kv) + [t_qsp[c]])
        dk = comp([dk0, dq])
        if c == 0:
            return dk, list(t_v0)
        dvs = []
        for hp in range(2):
            dvs.append(gdma(
                bass.AP(Vsb[c % 2], 66 * hp,
                        [[NT_F * VSP, 128], [VSP, NT_F], [1, 64]]),
                bass.AP(v_scr.tensor, (NT_F * 128 * 128) * c + 64 * hp,
                        [[128, 128], [128 * 128, NT_F], [1, 64]]),
                waits=wv))
        return dk, comp(dvs)

    t_pf01 = {}
    c_ones01 = {}
    for c in (0, 1):
        t_pf01[c] = prefetch_pair(c, ())

        def f_ones0(e, c=c):
            ones_ap = bass.AP(Vsb[c % 2], 64,
                              [[NT_F * VSP, 128], [VSP, NT_F], [66, 2], [1, 1]])
            e.memset(ones_ap, 1.0).then_inc(CV, 1)
        emitw("v", t_pf01[c][1], f_ones0)
        c_ones01[c] = pb.inc("cv")

    wcast_geo = [("ct", c_geo_t)] + comp([t_qsp[c] for c in range(8)])
    t_wout = comp([gdma(wout_bf[:, 1024 * k: 1024 * (k + 1)],
                        wout_h[128 * k: 128 * (k + 1), :], waits=wcast_geo)
                   for k in range(8)])
    t_wdT_toks = []
    for hc in range(24):
        t_wdT_toks.append(gdma(wdT[:, 1024 * hc: 1024 * (hc + 1)],
                               wdt_h[128 * hc: 128 * (hc + 1), :],
                               waits=wcast_geo))
    t_wdT = comp(t_wdT_toks)

    def f_warm1(e):
        ins = None
        for r in range(120):
            ins = e.matmul(PS[:, 512 * 6 + 128 * (r % 4):
                              512 * 6 + 128 * (r % 4) + 128],
                           ident_sb[:], stair_sb[:],
                           start=True, stop=True)
        ins.then_inc(CT, 1)
    emitw("t", [t_ident, ("cs", pb.get("cs"))], f_warm1)
    pb.inc("ct")

    pb.begin_switch()
    for arm in (0, 1):
        pb.begin_arm(arm)
        steps = []
        for pair in range(8):
            for wi, (q_lo, colbase) in enumerate(ARM_W512[arm]):
                ktmax = q_lo // 128 + 4
                for kt in range(ktmax):
                    v0 = max(0, 128 * kt - q_lo)
                    steps.append(dict(
                        pair=pair, wi=wi, q_lo=q_lo, colbase=colbase, kt=kt,
                        v0=v0, diag=(128 * kt >= q_lo), first=(kt == 0),
                        last=(kt == ktmax - 1), pw=4 * pair + wi))
        ns = len(steps)
        c_exp, c_mask, c_smm, c_avmm, c_norm = {}, {}, {}, {}, {}
        t_avdump = {}
        c_pair_last = {}
        c_ones = {}
        t_pf = {}
        t_pf[0], t_pf[1] = t_pf01[0], t_pf01[1]
        c_ones[0], c_ones[1] = c_ones01[0], c_ones01[1]

        def av_bank(st, hp):
            return PS[:, 512 * (4 + 2 * (st["wi"] % 2) + hp):
                      512 * (5 + 2 * (st["wi"] % 2) + hp)]

        for i in range(ns + 2):
            if i < ns:
                st = steps[i]
                pair = st["pair"]
                slot = i % 2
                waits = list(t_pf[pair][0])
                if i == 0:
                    waits.append(("cs", c_geo_qpe))
                if i >= 2:
                    waits.append(("cs", c_exp[i - 2]))

                def f_smm(e, st=st, slot=slot, pair=pair):
                    v0 = st["v0"]
                    ins = None
                    for hp in range(2):
                        r0 = 64 * hp
                        ins = e.matmul(
                            psS[slot][:, 512 * hp + v0: 512 * (hp + 1)],
                            Ksb[pair % 2][r0: r0 + 64,
                                          128 * st["kt"]: 128 * (st["kt"] + 1)],
                            Qsb[pair % 2][r0: r0 + 64,
                                          st["colbase"] + v0: st["colbase"] + 512],
                            start=True, stop=True, tile_position=(r0, 0))
                    ins.then_inc(CT, 1)
                emitw("t", waits, f_smm)
                c_smm[i] = pb.inc("ct")

            j = i - 2
            if j >= 0:
                st = steps[j]
                pair = st["pair"]
                slot = j % 2
                waits = [("cs", c_exp[j])]
                if st["diag"]:
                    waits.append(("cv", c_mask[j]))
                if j == 0:
                    waits.append(("cs", max(tr_state["evac"])))
                if st["first"]:
                    if st["wi"] == 0:
                        waits.extend(t_pf[pair][1])
                        waits.append(("cv", c_ones[pair]))
                    for hp in range(2):
                        pk = (st["pw"] - 2, hp)
                        if pk in c_norm:
                            waits.append(("cv", c_norm[pk]))
                qt0 = st["v0"] // 128

                def f_avmm(e, st=st, slot=slot, qt0=qt0, pair=pair, j=j):
                    kt, q_lo = st["kt"], st["q_lo"]
                    ins = None
                    for hp in range(2):
                        ps = av_bank(st, hp)
                        for qt in range(qt0, 4):
                            start = st["first"] and qt == 0
                            stop = qt == 3 and st["last"]
                            ins = e.matmul(
                                ps[:, 128 * qt: 128 * qt + 65],
                                Pt[j % 3][:, 512 * hp + 128 * qt: 512 * hp + 128 * (qt + 1)],
                                Vsb[pair % 2][:, VSP * kt + 66 * hp:
                                              VSP * kt + 66 * hp + 65],
                                start=start, stop=stop)
                    ins.then_inc(CT, 1)
                emitw("t", waits, f_avmm)
                c_avmm[j] = pb.inc("ct")

            if i < ns:
                st = steps[i]
                slot = i % 2
                pslot3 = i % 3
                waits = [("ct", c_smm[i])]
                if i >= 3:
                    waits.append(("ct", c_avmm[i - 3]))
                    if steps[i - 3]["diag"]:
                        waits.append(("cv", c_mask[i - 3]))

                def f_exp(e, st=st, slot=slot, pslot3=pslot3):
                    v0 = st["v0"]
                    in3 = psS[slot][:].rearrange("p (h w) -> p h w", h=2)[:, :, v0:512]
                    out3 = Pt[pslot3][:].rearrange("p (h w) -> p h w", h=2)[:, :, v0:512]
                    e.activation(out3, in3, AF.Exp, bias=-EXPC).then_inc(CS, 1)
                emitw("s", waits, f_exp)
                c_exp[i] = pb.inc("cs")

                if st["diag"]:
                    def f_mask(e, st=st, pslot3=pslot3):
                        v0 = st["v0"]
                        seg = Pt[pslot3][:].rearrange("p (h w) -> p h w", h=2)[:, :, v0:v0 + 128]
                        msk = stair_sb[:].unsqueeze(1).to_broadcast([128, 2, 128])
                        e.tensor_mul(seg, seg, msk).then_inc(CV, 1)
                    emitw("v", [("cs", c_exp[i])], f_mask)
                    c_mask[i] = pb.inc("cv")

            if j >= 0 and steps[j]["last"]:
                st = steps[j]
                pw, pair, wi = st["pw"], st["pair"], st["wi"]
                pslot = pair % 2
                for hp in range(2):
                    ps = av_bank(st, hp)
                    rcol = 8 * (st["wi"] % 2) + 4 * hp
                    waits = [("ct", c_avmm[j])]
                    pk = (pw - 2, hp)
                    if pk in c_norm:
                        waits.append(("cv", c_norm[pk]))
                    if hp == 0 and pair >= 2:
                        tok = t_avdump.get((pair - 2, wi))
                        if tok:
                            waits.append(tok)

                    def f_norm(e, ps=ps, rcol=rcol):
                        den = ps.rearrange("p (q w) -> p q w", w=128)[:, :, 64:65].squeeze(2)
                        e.reciprocal(recip_sb[:, rcol: rcol + 4], den).then_inc(CV, 1)
                    emitw("v", waits, f_norm)
                    c_rc = pb.inc("cv")

                    def f_nev(e, st=st, ps=ps, pslot=pslot, hp=hp, rcol=rcol):
                        out3 = AVpair[pslot][:].rearrange(
                            "p (tt w) -> p tt w", w=128)[:, 4 * st["wi"]: 4 * st["wi"] + 4,
                                                         64 * hp: 64 * hp + 64]
                        in3 = ps.rearrange("p (q w) -> p q w", w=128)[:, :, 0:64]
                        rec = recip_sb[:, rcol: rcol + 4].unsqueeze(2).to_broadcast([128, 4, 64])
                        e.tensor_tensor(out3, in3, rec, op=ALU.mult).then_inc(CV, 1)
                    emitw("v", [("cv", c_rc)], f_nev)
                    c_norm[(pw, hp)] = pb.inc("cv")

                toks = []
                for tt in range(4 * wi, 4 * wi + 4):
                    w = [("cv", c_norm[(pw, 1)])]
                    if tt == 4 * wi and pair >= 2:
                        w.append(t_avdump.get((pair - 2, wi), None))
                    if tt > 4 * wi:
                        w = ()
                    toks.append(dma(AVst[pslot][:, 128 * tt: 128 * (tt + 1)],
                                    AVpair[pslot][:, 128 * tt: 128 * (tt + 1)],
                                    transpose=True, waits=[x for x in w if x]))
                t_avdump[(pair, wi)] = dma(
                    bass.AP(av_scr.tensor, 128 * NOWN * pair + 512 * wi,
                            [[NOWN, 128], [1, 512]]),
                    AVst[pslot][:, 512 * wi: 512 * (wi + 1)],
                    waits=comp(toks))

                if wi == 3:
                    c_pair_last[pair] = c_avmm[j]
                    nxt = pair + 2
                    if nxt < 8:
                        t_pf[nxt] = prefetch_pair(nxt, [("ct", c_pair_last[pair])])
                        c_ones[nxt] = c_ones01[nxt % 2]
        pb.end_arm()
    pb.end_switch()

    def f_warm2(e):
        ins = None
        for r in range(120):
            ins = e.matmul(PS[:, 128 * (r % 4): 128 * (r % 4) + 128],
                           ident_sb[:], stair_sb[:],
                           start=True, stop=True)
        ins.then_inc(CT, 1)
    emitw("t", [("cs", pb.get("cs"))], f_warm2)
    pb.inc("ct")

    barrier()
    sb_free({"recip_sb", "AVst0", "AVst1", "AVpair0", "AVpair1", "Pt0", "Pt1",
             "Pt2", "Vsb1", "Qsb0", "Qsb1", "Ksb0", "Ksb1"})

    xnew = sbuf("xnew", [128, NT_O * D], F32)
    avt = [sbuf(f"avt{s}", [128, 1024], BF16) for s in range(2)]
    xo2 = [sbuf(f"xo2{s}", [128, D], F32) for s in range(2)]

    psO = [bank(0, 2), bank(2, 2)]
    ht2 = [sbuf(f"ht2{s}", [128, D], BF16) for s in range(2)]
    sq2 = sbuf("sq2", [128, D], F32)
    st2b = sbuf("st2b", [128, 16], F32)
    hz2 = {"tr": [[], []], "trev": [0, 0], "stat_rd": [0, 0], "sq_wr": 0}
    t_xn = {}
    c_o_evac = {}
    c_omm_hist = {}
    pend_tr2 = []

    def emit_tr2(t, slot, c_ht2):
        ptr = PS[:].bitcast(BF16)[:, 1024 * (4 + slot): 1024 * (5 + slot)]
        wl = [t_ident, ("cv", c_ht2)]
        if hz2["trev"][slot]:
            wl.append(("cv", hz2["trev"][slot]))

        def f_tr2(e, slot=slot, ptr=ptr):
            ins = None
            for k in range(8):
                ins = e.transpose(ptr[:, 128 * k: 128 * (k + 1)],
                                  ht2[slot][:, 128 * k: 128 * (k + 1)],
                                  ident_sb[:])
            ins.then_inc(CT, 1)
        emitw("t", wl, f_tr2)
        c_tr2 = pb.inc("ct")
        hz2["tr"][slot] = [("ct", c_tr2)]

        def f_tr2ev(e, t=t, ptr=ptr):
            out3 = h2[:].rearrange("p (k w) -> p k w", k=8)[:, :, 128 * t: 128 * (t + 1)]
            in3 = ptr.rearrange("p (k w) -> p k w", k=8)
            e.tensor_copy(out3, in3).then_inc(CV, 1)
        emitw("v", [("ct", c_tr2)], f_tr2ev)
        hz2["trev"][slot] = pb.inc("cv")
    for t in range(NT_O):
        slot = t % 2
        w = [("cv", c_o_evac[t - 2])] if t >= 2 else ()
        t_xo = dma(xo2[slot][:], xo_h[128 * t: 128 * (t + 1), :], waits=w)
        wa = [("ct", c_omm_hist[t - 2])] if t >= 2 else ()
        t_av = gdma(
            bass.AP(avt[slot], 0, [[1024, 128], [128, 8], [1, 128]]),
            bass.AP(av_scr.tensor, 128 * t, [[NOWN, 128], [128 * NOWN, 8], [1, 128]]),
            waits=wa)

        wl = t_wout + [t_av] + ([("cv", c_o_evac.get(t - 2, 0))] if t >= 2 else [])

        def f_omm(e, t=t, slot=slot):
            ins = None
            for c in range(8):
                for half in range(2):
                    ins = e.matmul(
                        psO[slot][:, 512 * half: 512 * (half + 1)],
                        avt[slot][:, 128 * c: 128 * (c + 1)],
                        wout_bf[:, 1024 * c + 512 * half: 1024 * c + 512 * (half + 1)],
                        start=(c == 0), stop=(c == 7))
            ins.then_inc(CT, 1)
        emitw("t", wl, f_omm)
        c_omm = pb.inc("ct")
        c_omm_hist[t] = c_omm

        def f_oev(e, t=t, slot=slot):
            e.tensor_add(xnew[:, D * t: D * (t + 1)], xo2[slot][:],
                         psO[slot][:, 0:1024]).then_inc(CV, 1)
        emitw("v", [("ct", c_omm), t_xo], f_oev)
        c_o_evac[t] = pb.inc("cv")

        st = st2b[:, 8 * slot: 8 * slot + 8]
        xt = xnew[:, D * t: D * (t + 1)]
        ht = ht2[slot][:]
        t_xn[t] = dma(xn_scr[t], xt, waits=[("cv", c_o_evac[t])])
        emitw("v", [("cv", max(hz2["stat_rd"][slot], c_o_evac[t]))],
              lambda e, xt=xt, st=st: e.reduce_sum(
                  st[:, 0:1], xt, axis=AX.X).then_inc(CV, 1))
        c_sum = pb.inc("cv")
        emitw("s", [("cs", hz2["sq_wr"]), ("cv", c_o_evac[t])],
              lambda e, xt=xt, st=st: e.activation(
                  sq2[:], xt, AF.Square, accum_out=st[:, 1:2]).then_inc(CS, 1))
        c_sq = pb.inc("cs")
        hz2["sq_wr"] = c_sq

        def f_stats(e, st=st, ws=c_sum, wq=c_sq):
            e.wait_ge(CV, ws)
            e.wait_ge(CS, wq)
            e.tensor_scalar(st[:, 2:3], st[:, 0:1], 1.0 / D, None, op0=ALU.mult).then_inc(CV, 1)
            e.wait_ge(CV, ws + 1)
            e.tensor_mul(st[:, 3:4], st[:, 2:3], st[:, 2:3]).then_inc(CV, 1)
            e.wait_ge(CV, ws + 2)
            e.scalar_tensor_tensor(st[:, 4:5], st[:, 1:2], 1.0 / D, st[:, 3:4],
                                   op0=ALU.mult, op1=ALU.subtract).then_inc(CV, 1)
        pb.emit("v", f_stats)
        pb.inc("cv", 2)
        c_var = pb.inc("cv")
        emitw("s", [("cv", c_var)],
              lambda e, st=st: e.activation(st[:, 5:6], st[:, 4:5], AF.Sqrt,
                                            bias=EPS).then_inc(CS, 1))
        c_std = pb.inc("cs")
        emitw("v", [("cs", c_std)],
              lambda e, st=st: e.reciprocal(st[:, 6:7], st[:, 5:6]).then_inc(CV, 1))
        c_r = pb.inc("cv")
        wl = [("cv", c_r)] + (hz2["tr"][slot] or [])
        emitw("v", wl,
              lambda e, st=st, xt=xt, ht=ht: e.tensor_scalar(
                  ht, xt, st[:, 2:3], st[:, 6:7],
                  op0=ALU.subtract, op1=ALU.mult).then_inc(CV, 1))
        c_ht2 = pb.inc("cv")
        hz2["stat_rd"][slot] = c_ht2
        pend_tr2.append((t, slot, c_ht2))
        if len(pend_tr2) >= 2 or t == NT_O - 1:
            for (tt_, slot_, cht_) in pend_tr2 if t == NT_O - 1 else [pend_tr2.pop(0)]:
                emit_tr2(tt_, slot_, cht_)
            if t == NT_O - 1:
                pend_tr2.clear()

    t_tr2 = [("cv", max(hz2["trev"]))]
    barrier()
    sb_free({"st2b", "sq2", "ht20", "ht21", "xo20", "xo21", "avt0", "avt1", "xnew",
             "wout_bf"})

    hid = [sbuf(f"hid{s}", [128, 24 * 1024], BF16) for s in range(2)]
    sg = sbuf("sg", [128, 1024], BF16)
    xnt = sbuf("xnt", [128, D], F32)
    yst = sbuf("yst", [128, D], F32)

    psG = [bank(0, 2), bank(4, 2)]
    psU = [bank(2, 2), bank(6, 2)]

    c_gu = {}
    c_sg = {}
    c_hid = {}
    for hc in range(24):
        slot = hc % NWS
        wcast = [("ct", c_gu[(hc - NWS, 1)])] if hc >= NWS else []
        wgu_in = bass.AP(wgu_h.tensor, 1024 * 256 * hc,
                         [[256, 128], [128 * 256, 8], [1, 256]])
        t_cgu = gdma(wguT[slot][:].rearrange("p (k w) -> p k w", k=8),
                     wgu_in, waits=wcast)

        for sbi in range(2):
            wl = [t_cgu] + t_tr2
            if (hc - 1, sbi) in c_hid:
                wl.append(("cv", c_hid[(hc - 1, sbi)]))

            def f_gu(e, sbi=sbi, slot=slot):
                ins = None
                for half in range(2):
                    for k in range(8):
                        ins = e.matmul(
                            psG[sbi][:, 512 * half: 512 * (half + 1)],
                            wguT[slot][:, 256 * k: 256 * k + 128],
                            h2[:, 2048 * k + 1024 * sbi + 512 * half:
                               2048 * k + 1024 * sbi + 512 * (half + 1)],
                            start=(k == 0), stop=(k == 7))
                for half in range(2):
                    for k in range(8):
                        ins = e.matmul(
                            psU[sbi][:, 512 * half: 512 * (half + 1)],
                            wguT[slot][:, 256 * k + 128: 256 * (k + 1)],
                            h2[:, 2048 * k + 1024 * sbi + 512 * half:
                               2048 * k + 1024 * sbi + 512 * (half + 1)],
                            start=(k == 0), stop=(k == 7))
                ins.then_inc(CT, 1)
            emitw("t", wl, f_gu)
            c_gu[(hc, sbi)] = pb.inc("ct")

            wl = [("ct", c_gu[(hc, sbi)])]
            if (hc - 1, sbi) in c_hid:
                wl.append(("cv", c_hid[(hc - 1, sbi)]))
            prev = (hc, 0) if sbi == 1 else (hc - 1, 1)
            if prev in c_hid:
                wl.append(("cv", c_hid[prev]))

            def f_silu(e, sbi=sbi):
                e.activation(sg[:], psG[sbi][:, 0:1024], AF.Silu,
                             scale=Q127).then_inc(CS, 1)
            emitw("s", wl, f_silu)
            c_sg[(hc, sbi)] = pb.inc("cs")

            def f_hf(e, hc=hc, sbi=sbi):
                e.scalar_tensor_tensor(hid[sbi][:, 1024 * hc: 1024 * (hc + 1)],
                                       psU[sbi][:, 0:1024], Q127, sg[:],
                                       op0=ALU.mult, op1=ALU.mult).then_inc(CV, 1)
            emitw("v", [("cs", c_sg[(hc, sbi)])], f_hf)
            c_hid[(hc, sbi)] = pb.inc("cv")

    psD = [bank(0, 2), bank(2, 2)]
    t_out_last = None
    c_dn_ev = {}
    t_y_hist = {}
    for t in range(NT_O):
        sbi, tt = divmod(t, 8)
        slot = t % 2
        wxn = [t_xn[t]]
        if t >= 1:
            wxn.append(("cv", c_dn_ev[t - 1]))
        t_xnt = dma(xnt[:], xn_scr[t], waits=wxn)

        wl = [("cv", c_hid[(23, sbi)]), ("cv", c_hid[(23, 0)])] + t_wdT
        if t >= 2:
            wl.append(("cv", c_dn_ev[t - 2]))

        def f_dn(e, sbi=sbi, tt=tt, slot=slot):
            ins = None
            for hc in range(24):
                for half in range(2):
                    ins = e.matmul(
                        psD[slot][:, 512 * half: 512 * (half + 1)],
                        hid[sbi][:, 1024 * hc + 128 * tt: 1024 * hc + 128 * (tt + 1)],
                        wdT[:, 1024 * hc + 512 * half: 1024 * hc + 512 * (half + 1)],
                        start=(hc == 0), stop=(hc == 23))
            ins.then_inc(CT, 1)
        emitw("t", wl, f_dn)
        c_dmm = pb.inc("ct")

        wl = [("ct", c_dmm), t_xnt]
        if t_y_hist.get(t - 1):
            wl.append(t_y_hist[t - 1])

        def f_dev(e, slot=slot):
            e.scalar_tensor_tensor(yst[:], psD[slot][:, 0:1024], Q127,
                                   xnt[:],
                                   op0=ALU.mult, op1=ALU.add).then_inc(CV, 1)
        emitw("v", wl, f_dev)
        c_dn_ev[t] = pb.inc("cv")
        t_out_last = dma(y_h[128 * t: 128 * (t + 1), :], yst[:],
                         waits=[("cv", c_dn_ev[t])])
        t_y_hist[t] = t_out_last

    emitw("y", [t_out_last], lambda e: None)

    def replay(ekey, eng):
        for it in pb.items[ekey]:
            if isinstance(it, SwitchItem):
                if not (it.arms[0][ekey] or it.arms[1][ekey]):
                    continue
                par = eng.snap(eng.partition_id() % 2)
                for a in nc.Switch(engines=[eng], index=[par], n=2):
                    for f in it.arms[a][ekey]:
                        f(eng)
            else:
                it(eng)

    with nc.Block() as blk:
        @blk.tensor
        def _(e):
            replay("t", e)

        @blk.scalar
        def _(e):
            replay("s", e)

        @blk.vector
        def _(e):
            replay("v", e)

        @blk.sync
        def _(e):
            replay("y", e)

        @blk.gpsimd
        def _(e):
            replay("g", e)

    return nc


_CACHED = {}


def _get_nc():
    if "nc" not in _CACHED:
        _CACHED["nc"] = build_program()
    return _CACHED["nc"]


def _own_idx(p):
    return np.r_[0:1024, 3072:4096] if p == 0 else np.r_[1024:3072]


def make_in_maps(x, wqkv, wout, hs, hd, wg, wu, wd):
    dgeo = np.zeros((128, 16), np.float32)
    hsq = np.zeros((16, 128), np.float32)
    dq = np.zeros((16, 128), np.float32)
    for h in range(16):
        m, j = h // 2, h % 2
        dgeo[64 * j + 8 * m: 64 * j + 8 * m + 8, h] = hd[h]
        hsq[h, 8 * h: 8 * h + 8] = hs[h]
        dq[h, 8 * h: 8 * h + 8] = hd[h]
    cst = np.ascontiguousarray(
        np.tile(np.asarray(CONST_VALS, np.float32)[None, :], (128, 1)))
    kk, jj = np.meshgrid(np.arange(128), np.arange(128), indexing="ij")
    stair = np.ascontiguousarray((jj >= kk).astype(BF16NP))
    ident = np.ascontiguousarray(np.eye(128).astype(BF16NP))
    wgu = np.ascontiguousarray(np.concatenate(
        [wg.T.reshape(D, 24, 128).transpose(1, 0, 2),
         wu.T.reshape(D, 24, 128).transpose(1, 0, 2)], axis=2))
    wdt = np.ascontiguousarray(wd.T)
    in_maps = []
    for c in range(8):
        b, p = c // 2, c % 2
        in_maps.append({
            "xf": x[b],
            "xo": np.ascontiguousarray(x[b][_own_idx(p)]),
            "wqkv": wqkv, "wout": wout,
            "dgeo": dgeo, "hsq": hsq, "dq": dq,
            "wgu": wgu, "wdt": wdt,
            "cst": cst, "stair": stair, "ident": ident,
        })
    return in_maps


def kernel(L0_attn_x, L0_attn_w_qkv, L0_attn_w_out, L0_attn_head_scales,
           L0_attn_head_dirs, L0_ffn_w_gate, L0_ffn_w_up, L0_ffn_w_down):
    x = np.ascontiguousarray(np.asarray(L0_attn_x, dtype=np.float32))
    wqkv = np.ascontiguousarray(np.asarray(L0_attn_w_qkv, dtype=np.float32))
    wout = np.ascontiguousarray(np.asarray(L0_attn_w_out, dtype=np.float32))
    hs = np.ascontiguousarray(np.asarray(L0_attn_head_scales, dtype=np.float32))
    hd = np.ascontiguousarray(np.asarray(L0_attn_head_dirs, dtype=np.float32))
    wg = np.ascontiguousarray(np.asarray(L0_ffn_w_gate, dtype=np.int32))
    wu = np.ascontiguousarray(np.asarray(L0_ffn_w_up, dtype=np.int32))
    wd = np.ascontiguousarray(np.asarray(L0_ffn_w_down, dtype=np.int32))

    nc = _get_nc()
    in_maps = make_in_maps(x, wqkv, wout, hs, hd, wg, wu, wd)
    res = run_bass_kernel_spmd(nc, in_maps, core_ids=list(range(8)))
    out = np.empty((B, T, D), np.float32)
    for c in range(8):
        b, p = c // 2, c % 2
        out[b][_own_idx(p)] = np.asarray(res.results[c]["y"], dtype=np.float32)
    return out


# version 36 (final)
# speedup vs baseline: 1.0106x; 1.0106x over previous
import numpy as np
import ml_dtypes

import concourse.bass as bass
import concourse.mybir as mybir
from concourse.bass_utils import run_bass_kernel_spmd

F32 = mybir.dt.float32
BF16 = mybir.dt.bfloat16
I32 = mybir.dt.int32
AF = mybir.ActivationFunctionType
ALU = mybir.AluOpType
AX = mybir.AxisListType

BF16NP = ml_dtypes.bfloat16

B, T, D = 4, 4096, 1024
H = 16
HID = 3072
NOWN = 2048
SCALE = 0.125
EXPC = 12.0
EPS = 1e-5
Q127 = 1.0 / 127.0

NT_F = T // 128
NT_O = NOWN // 128
GSZ = 2
NG_F = NT_F // GSZ
NG_O = NT_O // GSZ
VS = 66
VSP = 132

NYS = 16
NGS = 4

CONST_VALS = [-EXPC, EPS, 1.0 / D, Q127, 8.0, SCALE, 1.0, 0.5]

ARM_WINDOWS = {0: [(0, 0), (3072, 1024)], 1: [(1024, 0), (2048, 1024)]}
ARM_W512 = {
    0: [(0, 0), (512, 512), (3072, 1024), (3584, 1536)],
    1: [(1024, 0), (1536, 512), (2048, 1024), (2560, 1536)],
}

ENGS = ("t", "s", "v", "y", "g")


class SwitchItem:
    def __init__(self):
        self.arms = {0: {e: [] for e in ENGS}, 1: {e: [] for e in ENGS}}


SEM_OWNER = {"ct": "t", "cs": "s", "cv": "v"}


class PB:
    def __init__(self):
        self.items = {e: [] for e in ENGS}
        self.cnt = {}
        self.waited = {e: {} for e in ENGS}
        self._arm = None
        self._switch_item = None
        self._arm_snapshot = None
        self._arm_counts = None
        self._arm_waited = None
        self._arm_waited_snap = None

    def emit(self, e, f):
        if self._arm is not None:
            si, a = self._arm
            si.arms[a][e].append(f)
        else:
            self.items[e].append(f)

    def filter_waits(self, e, waits):
        kept = []
        w = self.waited[e]
        for s, v in waits:
            if not v:
                continue
            if v <= w.get(s, 0):
                continue
            w[s] = v
            kept.append((s, v))
        return kept

    def inc(self, sem, by=1):
        self.cnt[sem] = self.cnt.get(sem, 0) + by
        return self.cnt[sem]

    def get(self, sem):
        return self.cnt.get(sem, 0)

    def begin_switch(self):
        si = SwitchItem()
        for e in ENGS:
            self.items[e].append(si)
        self._switch_item = si
        self._arm_snapshot = dict(self.cnt)
        self._arm_waited_snap = {e: dict(self.waited[e]) for e in ENGS}
        self._arm_counts = []
        self._arm_waited = []

    def begin_arm(self, a):
        self.cnt = dict(self._arm_snapshot)
        self.waited = {e: dict(self._arm_waited_snap[e]) for e in ENGS}
        self._arm = (self._switch_item, a)

    def end_arm(self):
        self._arm_counts.append(dict(self.cnt))
        self._arm_waited.append(self.waited)
        self._arm = None

    def end_switch(self):
        c0, c1 = self._arm_counts
        assert c0 == c1, f"arm sem counts differ:\n{c0}\n{c1}"
        self.cnt = c0
        w0, w1 = self._arm_waited
        self.waited = {
            e: {s: min(w0[e][s], w1[e][s]) for s in w0[e] if s in w1[e]}
            for e in ENGS}
        self._switch_item = None


def build_program():
    nc = bass.Bass("TRN2", target_bir_lowering=False)

    xf_h = nc.dram_tensor("xf", [T, D], F32, kind="ExternalInput").ap()
    xo_h = nc.dram_tensor("xo", [NOWN, D], F32, kind="ExternalInput").ap()
    wqkv_h = nc.dram_tensor("wqkv", [D, 3 * D], F32, kind="ExternalInput").ap()
    wout_h = nc.dram_tensor("wout", [D, D], F32, kind="ExternalInput").ap()
    dgeo_h = nc.dram_tensor("dgeo", [128, 16], F32, kind="ExternalInput").ap()
    hsq_h = nc.dram_tensor("hsq", [16, 128], F32, kind="ExternalInput").ap()
    dq_h = nc.dram_tensor("dq", [16, 128], F32, kind="ExternalInput").ap()
    wgu_h = nc.dram_tensor("wgu", [24, D, 256], I32, kind="ExternalInput").ap()
    wdt_h = nc.dram_tensor("wdt", [HID, D], I32, kind="ExternalInput").ap()
    cst_h = nc.dram_tensor("cst", [128, len(CONST_VALS)], F32, kind="ExternalInput").ap()
    stair_h = nc.dram_tensor("stair", [128, 128], BF16, kind="ExternalInput").ap()
    ident_h = nc.dram_tensor("ident", [128, 128], BF16, kind="ExternalInput").ap()
    y_h = nc.dram_tensor("y", [NOWN, D], F32, kind="ExternalOutput").ap()

    k_scr = nc.dram_tensor("k_scr", [8, 128, T], BF16, kind="Internal").ap()
    v_scr = nc.dram_tensor("v_scr", [8, NT_F, 128, 128], BF16, kind="Internal").ap()
    av_scr = nc.dram_tensor("av_scr", [8, 128, NOWN], BF16, kind="Internal").ap()
    xn_scr = nc.dram_tensor("xn_scr", [NT_O, 128, D], F32, kind="Internal").ap()
    q_scr = nc.dram_tensor("q_scr", [8, 128, NOWN], BF16, kind="Internal").ap()

    pb = PB()

    SEM = {}
    for name in (["ct", "cs", "cv", "bar"] + [f"dy{i}" for i in range(NYS)]
                 + [f"dg{i}" for i in range(NGS)]):
        SEM[name] = nc.semaphore(name).__enter__()
    CT, CS, CV, BAR = SEM["ct"], SEM["cs"], SEM["cv"], SEM["bar"]

    alive = []

    def sbuf(name, shape, dt):
        gd = nc.sbuf_tensor(name, shape, dt)
        t = gd.__enter__()
        alive.append((name, gd))
        return t

    def sb_free(names):
        nset = set(names)
        while nset:
            name, gd = alive[-1]
            assert name in nset, f"non-LIFO free: top={name}, want={nset}"
            gd.__exit__(None, None, None)
            alive.pop()
            nset.remove(name)

    cst_sb = sbuf("cst_sb", [128, len(CONST_VALS)], F32)
    stair_sb = sbuf("stair_sb", [128, 128], BF16)
    ident_sb = sbuf("ident_sb", [128, 128], BF16)
    dgeo_sb = sbuf("dgeo_sb", [128, 16], F32)
    dgeo_bf = sbuf("dgeo_bf", [128, 16], BF16)
    hsq_sb = sbuf("hsq_sb", [16, 128], F32)
    dq_sb = sbuf("dq_sb", [16, 128], F32)
    sdm = sbuf("sdm", [16, 128], BF16)

    for i, val in enumerate(CONST_VALS):
        nc.const_aps.register(F32, val, cst_sb[:, i:i + 1])

    ps_guard = nc.psum_tensor("ps_all", [128, 4096], F32)
    PS = ps_guard.__enter__()

    def bank(b0, nb=1):
        return PS[:, 512 * b0: 512 * (b0 + nb)]

    def emitw(e, waits, f):
        wl = tuple((SEM[s], v) for (s, v) in pb.filter_waits(e, waits))
        def g(eng, wl=wl, f=f):
            for s, v in wl:
                eng.wait_ge(s, v)
            f(eng)
        pb.emit(e, g)

    def dma(out_ap, in_ap, transpose=False, waits=()):
        i = pb.get("_yrr") % NYS
        pb.inc("_yrr")
        name = f"dy{i}"
        pre = pb.get(name)
        post = pb.inc(name)
        wl = [(name, 16 * pre)] if pre else []
        wl += [w for w in waits if w and w[1]]
        wlh = tuple((SEM[s], v) for (s, v) in pb.filter_waits("y", wl))
        def f(eng, o=out_ap, i_=in_ap, tr=transpose, wlh=wlh, sem=SEM[name]):
            for s, v in wlh:
                eng.wait_ge(s, v)
            eng.dma_start(out=o, in_=i_, transpose=tr).then_inc(sem, 16)
        pb.emit("y", f)
        return (name, 16 * post)

    def gdma(out_ap, in_ap, waits=()):
        i = pb.get("_grr") % NGS
        pb.inc("_grr")
        name = f"dg{i}"
        pre = pb.get(name)
        post = pb.inc(name)
        wl = [(name, 16 * pre)] if pre else []
        wl += [w for w in waits if w and w[1]]
        wlh = tuple((SEM[s], v) for (s, v) in pb.filter_waits("g", wl))
        def f(eng, o=out_ap, i_=in_ap, wlh=wlh, sem=SEM[name]):
            for s, v in wlh:
                eng.wait_ge(s, v)
            eng.dma_start(out=o, in_=i_).then_inc(sem, 16)
        pb.emit("g", f)
        return (name, 16 * post)

    def comp(tokens):
        d = {}
        for tk in tokens:
            if not tk:
                continue
            n, v = tk
            d[n] = max(d.get(n, 0), v)
        return [(n, v) for n, v in d.items()]

    def barrier():
        ywaits = [(f"dy{i}", 16 * pb.get(f"dy{i}")) for i in range(NYS)]
        gwaits = [(f"dg{i}", 16 * pb.get(f"dg{i}")) for i in range(NGS)]
        own = {"t": [("ct", pb.get("ct"))], "s": [("cs", pb.get("cs"))],
               "v": [("cv", pb.get("cv"))], "y": ywaits, "g": gwaits}
        k = pb.inc("bar", 5) // 5
        for e in ENGS:
            def f(eng, kk=k):
                eng.nop().then_inc(BAR, 1)
                eng.wait_ge(BAR, 5 * kk)
            emitw(e, own[e], f)
        allw = [w for lst in own.values() for w in lst]
        for e in ENGS:
            for s, v in allw:
                if v > pb.waited[e].get(s, 0):
                    pb.waited[e][s] = v

    t_cst = dma(cst_sb[:], cst_h[:])
    dma(stair_sb[:], stair_h[:])
    t_ident = dma(ident_sb[:], ident_h[:])
    t_dgeo = dma(dgeo_sb[:], dgeo_h[:])
    t_hsq = dma(hsq_sb[:], hsq_h[:])
    t_dq = dma(dq_sb[:], dq_h[:])

    def f_sd(e):
        e.scalar_tensor_tensor(sdm[:], dq_sb[:], 8.0, hsq_sb[:],
                               op0=ALU.mult, op1=ALU.mult).then_inc(CV, 1)
        e.tensor_copy(dgeo_bf[:], dgeo_sb[:]).then_inc(CV, 1)
    emitw("v", [t_cst, t_dgeo, t_hsq, t_dq], f_sd)
    pb.inc("cv", 2)
    c_sd = pb.get("cv")

    Vsb0 = sbuf("Vsb0", [128, NT_F * VSP], BF16)

    NLS = 3
    xg = [sbuf(f"xg{s}", [128, GSZ * D], F32) for s in range(NLS)]
    htg = [sbuf(f"htg{s}", [128, GSZ * D], BF16) for s in range(NLS)]
    hfg = [sbuf(f"hfg{s}", [128, GSZ * D], BF16) for s in range(NLS)]
    sqd = sbuf("sqd", [128, D], F32)
    statb = sbuf("statb", [128, 3 * 8], F32)
    Kst = [sbuf(f"Kst{s}", [128, 2048], BF16) for s in range(2)]
    Vst = [sbuf(f"Vst{s}", [128, 1024], BF16) for s in range(2)]
    wkv = sbuf("wkv", [128, 8 * 2048], BF16)

    t_wkv = comp([gdma(wkv[:, 2048 * k: 2048 * (k + 1)],
                       wqkv_h[128 * k: 128 * (k + 1), D: 3 * D])
                  for k in range(8)])

    hz = {
        "stat_rd": [0] * NLS, "sq_wr": 0,
        "xg_rd_v": [0] * NLS, "xg_rd_s": [0] * NLS,
        "ht_tr": [[] for _ in range(NLS)], "hf_rd": [0] * NLS,
    }

    def ln_group(slot, t_loads):
        st = statb[:, 8 * slot: 8 * slot + 8]
        c_ht = 0
        c_hts = []
        for i in range(GSZ):
            t_load = t_loads[i]
            xt = xg[slot][:, D * i: D * (i + 1)]
            ht = htg[slot][:, D * i: D * (i + 1)]
            emitw("v", [t_load, ("cv", hz["stat_rd"][slot])],
                  lambda e, xt=xt, st=st: e.reduce_sum(
                      st[:, 0:1], xt, axis=AX.X).then_inc(CV, 1))
            c_sum = pb.inc("cv")
            emitw("s", [t_load, ("cs", hz["sq_wr"])],
                  lambda e, xt=xt, st=st: e.activation(
                      sqd[:], xt, AF.Square, accum_out=st[:, 1:2]).then_inc(CS, 1))
            c_sq = pb.inc("cs")
            hz["sq_wr"] = c_sq

            def f_stats(e, st=st, ws=c_sum, wq=c_sq):
                e.wait_ge(CV, ws)
                e.wait_ge(CS, wq)
                e.tensor_scalar(st[:, 2:3], st[:, 0:1], 1.0 / D, None,
                                op0=ALU.mult).then_inc(CV, 1)
                e.wait_ge(CV, ws + 1)
                e.tensor_mul(st[:, 3:4], st[:, 2:3], st[:, 2:3]).then_inc(CV, 1)
                e.wait_ge(CV, ws + 2)
                e.scalar_tensor_tensor(st[:, 4:5], st[:, 1:2], 1.0 / D, st[:, 3:4],
                                       op0=ALU.mult, op1=ALU.subtract).then_inc(CV, 1)
            pb.emit("v", f_stats)
            pb.inc("cv", 2)
            c_var = pb.inc("cv")
            emitw("s", [("cv", c_var)],
                  lambda e, st=st: e.activation(st[:, 5:6], st[:, 4:5], AF.Sqrt,
                                                bias=EPS).then_inc(CS, 1))
            c_std = pb.inc("cs")
            emitw("v", [("cs", c_std)],
                  lambda e, st=st: e.reciprocal(st[:, 6:7], st[:, 5:6]).then_inc(CV, 1))
            c_rstd = pb.inc("cv")
            wl = [("cv", c_rstd)] + (hz["ht_tr"][slot] or [])
            emitw("v", wl,
                  lambda e, st=st, xt=xt, ht=ht: e.tensor_scalar(
                      ht, xt, st[:, 2:3], st[:, 6:7],
                      op0=ALU.subtract, op1=ALU.mult).then_inc(CV, 1))
            c_ht = pb.inc("cv")
            c_hts.append(c_ht)
        hz["stat_rd"][slot] = c_ht
        hz["xg_rd_v"][slot] = c_ht
        hz["xg_rd_s"][slot] = hz["sq_wr"]
        return c_hts

    tr_state = {"evac": [0] * GSZ}

    def transpose_group(slot, c_hts, ps_base):
        c_tr = 0
        for i in range(GSZ):
            ptr = PS[:].bitcast(BF16)[:, 1024 * (ps_base + i): 1024 * (ps_base + i + 1)]
            wl = [t_ident, ("cv", c_hts[i])]
            if tr_state["evac"][i]:
                wl.append(("cs", tr_state["evac"][i]))

            def f_tr(e, slot=slot, i=i, ptr=ptr):
                ins = None
                for k in range(8):
                    ins = e.transpose(
                        ptr[:, 128 * k: 128 * (k + 1)],
                        htg[slot][:, 1024 * i + 128 * k: 1024 * i + 128 * (k + 1)],
                        ident_sb[:])
                ins.then_inc(CT, 1)
            emitw("t", wl, f_tr)
            c_tr = pb.inc("ct")

            def f_trev(e, slot=slot, i=i, ptr=ptr):
                out3 = hfg[slot][:].rearrange("p (k w) -> p k w", k=8)[:, :, 128 * i: 128 * (i + 1)]
                in3 = ptr.rearrange("p (k w) -> p k w", k=8)
                e.copy(out3, in3).then_inc(CS, 1)
            emitw("s", [("ct", c_tr), ("ct", hz["hf_rd"][slot])], f_trev)
            tr_state["evac"][i] = pb.inc("cs")
        hz["ht_tr"][slot] = [("ct", c_tr)]
        return [("cs", tr_state["evac"][GSZ - 1])]

    psK = bank(0, 4)
    psV = bank(4, 2)
    c_kv_evac = 0
    c_v_evac = 0
    t_kst = [None, None]
    t_vst = [None, None]

    ld_toks = {}
    tr_toks = {}

    def p1_load(g):
        slot = g % NLS
        toks = []
        for i in range(GSZ):
            t0 = GSZ * g + i
            w = [("cv", hz["xg_rd_v"][slot]), ("cs", hz["xg_rd_s"][slot])] if i == 0 else ()
            toks.append(dma(xg[slot][:, D * i: D * (i + 1)],
                            xf_h[128 * t0: 128 * (t0 + 1), :], waits=w))
        ld_toks[g] = toks

    def p1_ln(g):
        slot = g % NLS
        c_hts = ln_group(slot, ld_toks[g])
        hz["xg_rd_s"][slot] = hz["sq_wr"]
        return c_hts

    state = {"c_kv_evac": 0, "c_v_evac": 0,
             "t_kst": [None, None], "t_vst": [None, None]}
    ks_toks = []
    vs_toks = []

    def p1_kmm_ev(g):
        slot = g % NLS
        waits = tr_toks[g] + t_wkv + (
            [("cs", state["c_kv_evac"])] if state["c_kv_evac"] else [])

        def k_mms(e, slot=slot):
            ins = None
            for k in range(8):
                for m in range(8):
                    ins = e.matmul(psK[:, 256 * m: 256 * (m + 1)],
                                   wkv[:, 2048 * k + 128 * m: 2048 * k + 128 * (m + 1)],
                                   hfg[slot][:, 256 * k: 256 * (k + 1)],
                                   start=(k == 0 and m % 2 == 0),
                                   stop=(k == 7 and m % 2 == 1))
            ins.then_inc(CT, 1)
        emitw("t", waits, k_mms)
        c_kmm = pb.inc("ct")

        kslot = g % 2
        wl = [("ct", c_kmm)] + ([state["t_kst"][kslot]] if state["t_kst"][kslot] else [])
        emitw("s", wl, lambda e, kslot=kslot: e.copy(
            Kst[kslot][:], PS[:, 0:2048]).then_inc(CS, 1))
        state["c_kv_evac"] = pb.inc("cs")
        state["t_kst"][kslot] = dma(
            bass.AP(k_scr.tensor, 256 * g, [[T, 128], [128 * T, 8], [1, 256]]),
            bass.AP(Kst[kslot], 0, [[2048, 128], [256, 8], [1, 256]]),
            waits=[("cs", state["c_kv_evac"])])
        ks_toks.append(state["t_kst"][kslot])

    def p1_vmm_ev(g):
        slot = g % NLS
        c_vmm = []
        for i in range(GSZ):
            wlv = [("cs", state["c_v_evac"])] if state["c_v_evac"] else []

            def v_mms(e, slot=slot, i=i):
                ins = None
                for half in range(2):
                    for k in range(8):
                        ins = e.matmul(
                            psV[:, 512 * half: 512 * (half + 1)],
                            hfg[slot][:, 256 * k + 128 * i: 256 * k + 128 * (i + 1)],
                            wkv[:, 2048 * k + 1024 + 512 * half:
                                2048 * k + 1024 + 512 * (half + 1)],
                            start=(k == 0), stop=(k == 7))
                ins.then_inc(CT, 1)
            emitw("t", wlv, v_mms)
            c_vmm.append(pb.inc("ct"))
            t0 = GSZ * g + i
            vslot = t0 % 2
            wl = [("ct", c_vmm[i])]
            if state["c_v_evac"]:
                wl.append(("cs", state["c_v_evac"]))
            if state["t_vst"][vslot]:
                wl.append(state["t_vst"][vslot])
            emitw("s", wl, lambda e, vslot=vslot: e.copy(
                Vst[vslot][:], PS[:, 2048:3072]).then_inc(CS, 1))
            state["c_v_evac"] = pb.inc("cs")
            state["t_vst"][vslot] = dma(
                bass.AP(v_scr.tensor, 128 * 128 * t0,
                        [[128, 128], [NT_F * 128 * 128, 8], [1, 128]]),
                bass.AP(Vst[vslot], 0, [[1024, 128], [128, 8], [1, 128]]),
                waits=[("cs", state["c_v_evac"])])
            vs_toks.append(state["t_vst"][vslot])
        hz["hf_rd"][slot] = c_vmm[-1]

    hts = {}
    p1_load(0)
    p1_load(1)
    p1_load(2)
    hts[0] = p1_ln(0)
    tr_toks[0] = transpose_group(0 % NLS, hts[0], 6)
    hts[1] = p1_ln(1)
    for g in range(NG_F):
        if g + 3 < NG_F:
            p1_load(g + 3)
        if g + 2 < NG_F:
            hts[g + 2] = p1_ln(g + 2)
        if g + 1 < NG_F:
            tr_toks[g + 1] = transpose_group((g + 1) % NLS, hts[g + 1], 6)
        p1_kmm_ev(g)
        p1_vmm_ev(g)

    c_p1_last_mm = hz["hf_rd"][(NG_F - 1) % NLS]
    sb_free({"wkv"})



    Qt = sbuf("Qt", [128, 8 * NOWN], BF16)
    wq = sbuf("wq", [128, 8 * 1024], BF16)
    t_wq = comp([gdma(wq[:, 1024 * k: 1024 * (k + 1)],
                      wqkv_h[128 * k: 128 * (k + 1), 0:D]) for k in range(8)])

    t_v0 = comp([gdma(
        bass.AP(Vsb0, 66 * hp,
                [[NT_F * VSP, 128], [VSP, NT_F], [1, 64]]),
        bass.AP(v_scr.tensor, 64 * hp,
                [[128, 128], [128 * 128, NT_F], [1, 64]]),
        waits=comp(vs_toks)) for hp in range(2)])
    w8t = sbuf("w8t", [128, 8 * 128], BF16)
    u_sb = sbuf("u_sb", [16, 8 * 128], BF16)
    wstg = [sbuf(f"wstg{s}", [128, 128], BF16) for s in range(2)]

    ps_trf = [PS[:].bitcast(BF16)[:, 0:128],
              PS[:].bitcast(BF16)[:, 3 * 1024: 3 * 1024 + 128]]
    ps_uf = PS[0:16, 512:640]
    ps_ff = PS[:, 1024:1152]
    c_w8h = {}
    c_trfh = {}
    c_ue = 0
    c_fadd = 0
    for k in range(8):
        s2 = k % 2
        wl = t_wq + [("cv", c_sd)]
        if k >= 2:
            wl.append(("ct", c_trfh[k - 2]))

        def f_stg(e, k=k, s2=s2):
            for j in range(2):
                out3 = wstg[s2][:, 64 * j: 64 * (j + 1)].rearrange(
                    "p (m a) -> p m a", a=8)
                in3 = bass.AP(wq, 1024 * k + 64 * j,
                              [[8192, 128], [128, 8], [1, 8]])
                ins = e.tensor_copy(out3, in3)
            ins.then_inc(CV, 1)
        emitw("v", wl, f_stg)
        c_stg = pb.inc("cv")

        wl = [t_ident, ("cs", state["c_kv_evac"]), ("cv", c_stg)]
        if k >= 2:
            wl.append(("cs", c_w8h[k - 2]))
        if c_fadd:
            wl.append(("cv", c_fadd))

        def f_trf(e, k=k, s2=s2):
            e.transpose(ps_trf[s2][:], wstg[s2][:], ident_sb[:]).then_inc(CT, 1)
        emitw("t", wl, f_trf)
        c_trfh[k] = pb.inc("ct")

        def f_w8e(e, k=k, s2=s2):
            e.copy(w8t[:, 128 * k: 128 * (k + 1)], ps_trf[s2][:]).then_inc(CS, 1)
        emitw("s", [("ct", c_trfh[k])], f_w8e)
        c_w8h[k] = pb.inc("cs")
        c_w8 = c_w8h[k]

        wl = [("cs", c_w8)]
        if c_ue:
            wl.append(("cs", c_ue))

        def f_umm(e, k=k):
            e.matmul(ps_uf, dgeo_bf[:],
                     w8t[:, 128 * k: 128 * (k + 1)],
                     start=True, stop=True).then_inc(CT, 1)
        emitw("t", wl, f_umm)
        c_umm = pb.inc("ct")

        def f_ue(e, k=k):
            e.copy(u_sb[:, 128 * k: 128 * (k + 1)], ps_uf).then_inc(CS, 1)
        emitw("s", [("ct", c_umm)], f_ue)
        c_ue = pb.inc("cs")

        wl = [("cs", c_ue)]
        if c_fadd:
            wl.append(("cv", c_fadd))

        def f_fmm(e, k=k):
            e.matmul(ps_ff, u_sb[0:16, 128 * k: 128 * (k + 1)], sdm[:],
                     start=True, stop=True).then_inc(CT, 1)
        emitw("t", wl, f_fmm)
        c_fmm = pb.inc("ct")

        for j in range(2):
            def f_fadd(e, k=k, j=j):
                wcols = bass.AP(wq, 1024 * k + 64 * j,
                                [[8192, 128], [128, 8], [1, 8]])
                pcols = bass.AP(PS, 1024 + 8 * j,
                                [[4096, 128], [16, 8], [1, 8]])
                e.tensor_add(wcols, wcols, pcols).then_inc(CV, 1)
            emitw("v", [("ct", c_fmm)], f_fadd)
            c_fadd = pb.inc("cv")
    c_fold = c_fadd

    psQ = bank(0, 4)
    q_state = {"c_q_evac": [state["c_kv_evac"], state["c_kv_evac"]]}
    tr_state["evac"] = [state["c_v_evac"]] * GSZ

    def p2_load(g):
        slot = g % NLS
        toks = []
        for i in range(GSZ):
            t0 = GSZ * g + i
            w = [("cv", hz["xg_rd_v"][slot]), ("cs", hz["xg_rd_s"][slot])] if i == 0 else ()
            toks.append(dma(xg[slot][:, D * i: D * (i + 1)],
                            xo_h[128 * t0: 128 * (t0 + 1), :], waits=w))
        ld_toks[g] = toks

    def p2_mm_ev(g):
        slot = g % NLS
        for h in range(2):
            waits = tr_toks[g] + t_wq + [("cv", c_fold)] + (
                [("cs", q_state["c_q_evac"][h])] if q_state["c_q_evac"][h] else [])

            def q_mms(e, slot=slot, h=h):
                ins = None
                for k in range(8):
                    for m in range(4 * h, 4 * h + 4):
                        ins = e.matmul(psQ[:, 256 * m: 256 * (m + 1)],
                                       wq[:, 1024 * k + 128 * m: 1024 * k + 128 * (m + 1)],
                                       hfg[slot][:, 256 * k: 256 * (k + 1)],
                                       start=(k == 0 and m % 2 == 0),
                                       stop=(k == 7 and m % 2 == 1))
                ins.then_inc(CT, 1)
            emitw("t", waits, q_mms)
            c_qmm = pb.inc("ct")
            if h == 1:
                hz["hf_rd"][slot] = c_qmm

            def q_evac(e, g=g, h=h):
                out3 = Qt[:].rearrange("p (m w) -> p m w", m=8)[:, 4 * h: 4 * h + 4,
                                                               256 * g: 256 * (g + 1)]
                in3 = PS[:, 1024 * h: 1024 * (h + 1)].rearrange("p (m w) -> p m w", m=4)
                e.activation(out3, in3, AF.Copy, scale=SCALE).then_inc(CS, 1)
            emitw("s", [("ct", c_qmm)], q_evac)
            q_state["c_q_evac"][h] = pb.inc("cs")

    def p2_ln(g):
        slot = g % NLS
        c = ln_group(slot, ld_toks[g])
        hz["xg_rd_s"][slot] = hz["sq_wr"]
        return c

    hts = {}
    p2_load(0)
    p2_load(1)
    p2_load(2)
    hts[0] = p2_ln(0)
    tr_toks[0] = transpose_group(0 % NLS, hts[0], 4)
    hts[1] = p2_ln(1)
    for g in range(NG_O):
        if g + 3 < NG_O:
            p2_load(g + 3)
        if g + 2 < NG_O:
            hts[g + 2] = p2_ln(g + 2)
        if g + 1 < NG_O:
            tr_toks[g + 1] = transpose_group((g + 1) % NLS, hts[g + 1], 4)
        p2_mm_ev(g)

    barrier()

    c_qe_fin = max(q_state["c_q_evac"])
    c_qmm_fin = pb.get("ct")
    t_qsp = {}
    for c in range(8):
        du = dma if c < 2 else gdma
        t_qsp[c] = du(q_scr[c], Qt[:, 2048 * c: 2048 * (c + 1)],
                      waits=[("cs", c_qe_fin)])

    c_geo_t = c_qmm_fin
    c_geo_qpe = c_qe_fin
    c_r1_add = 0
    sb_free({"wstg0", "wstg1", "u_sb", "w8t", "wq", "Qt", "Vst0", "Vst1", "Kst0", "Kst1", "statb",
             "sqd", "hfg0", "hfg1", "hfg2", "htg0", "htg1", "htg2",
             "xg0", "xg1", "xg2"})

    h2 = sbuf("h2", [128, 8 * NOWN], BF16)
    wdT = sbuf("wdT", [128, 24 * 1024], BF16)
    NWS = 2
    wguT = [sbuf(f"wguT{s}", [128, 8 * 256], BF16) for s in range(NWS)]

    wout_bf = sbuf("wout_bf", [128, 8 * 1024], BF16)

    Ksb = [sbuf(f"Ksb{s}", [128, T], BF16) for s in range(2)]
    Qsb = [sbuf(f"Qsb{s}", [128, NOWN], BF16) for s in range(2)]

    Vsb = [Vsb0, sbuf("Vsb1", [128, NT_F * VSP], BF16)]
    Pt = [sbuf(f"Pt{s}", [128, 1024], BF16) for s in range(3)]
    AVpair = [sbuf(f"AVpair{s}", [128, NOWN], BF16) for s in range(2)]
    AVst = [sbuf(f"AVst{s}", [128, NOWN], BF16) for s in range(2)]
    recip_sb = sbuf("recip_sb", [128, 16], F32)

    psS = [bank(0, 2), bank(2, 2)]
    psAV = [bank(4, 2), bank(6, 2)]

    def prefetch_pair(c, waits_kv):
        wk = list(waits_kv) + (comp(ks_toks) if c < 2 else [])
        wv = list(waits_kv) + (comp(vs_toks) if c < 2 else [])
        dk0 = dma(Ksb[c % 2][:], k_scr[c], waits=wk)
        dq = dma(Qsb[c % 2][:], q_scr[c], waits=list(waits_kv) + [t_qsp[c]])
        dk = comp([dk0, dq])
        if c == 0:
            return dk, list(t_v0)
        dvs = []
        for hp in range(2):
            dvs.append(gdma(
                bass.AP(Vsb[c % 2], 66 * hp,
                        [[NT_F * VSP, 128], [VSP, NT_F], [1, 64]]),
                bass.AP(v_scr.tensor, (NT_F * 128 * 128) * c + 64 * hp,
                        [[128, 128], [128 * 128, NT_F], [1, 64]]),
                waits=wv))
        return dk, comp(dvs)

    t_pf01 = {}
    c_ones01 = {}
    for c in (0, 1):
        t_pf01[c] = prefetch_pair(c, ())

        def f_ones0(e, c=c):
            ones_ap = bass.AP(Vsb[c % 2], 64,
                              [[NT_F * VSP, 128], [VSP, NT_F], [66, 2], [1, 1]])
            e.memset(ones_ap, 1.0).then_inc(CV, 1)
        emitw("v", t_pf01[c][1], f_ones0)
        c_ones01[c] = pb.inc("cv")

    wcast_geo = [("ct", c_geo_t)] + comp([t_qsp[c] for c in range(8)])
    t_wout = comp([gdma(wout_bf[:, 1024 * k: 1024 * (k + 1)],
                        wout_h[128 * k: 128 * (k + 1), :], waits=wcast_geo)
                   for k in range(8)])
    t_wdT_toks = []
    for hc in range(24):
        t_wdT_toks.append(gdma(wdT[:, 1024 * hc: 1024 * (hc + 1)],
                               wdt_h[128 * hc: 128 * (hc + 1), :],
                               waits=wcast_geo))
    t_wdT = comp(t_wdT_toks)

    pb.begin_switch()
    for arm in (0, 1):
        pb.begin_arm(arm)
        steps = []
        for pair in range(8):
            for wi, (q_lo, colbase) in enumerate(ARM_W512[arm]):
                ktmax = q_lo // 128 + 4
                for kt in range(ktmax):
                    v0 = max(0, 128 * kt - q_lo)
                    steps.append(dict(
                        pair=pair, wi=wi, q_lo=q_lo, colbase=colbase, kt=kt,
                        v0=v0, diag=(128 * kt >= q_lo), first=(kt == 0),
                        last=(kt == ktmax - 1), pw=4 * pair + wi))
        ns = len(steps)
        c_exp, c_mask, c_smm, c_avmm, c_norm = {}, {}, {}, {}, {}
        t_avdump = {}
        c_pair_last = {}
        c_ones = {}
        t_pf = {}
        t_pf[0], t_pf[1] = t_pf01[0], t_pf01[1]
        c_ones[0], c_ones[1] = c_ones01[0], c_ones01[1]

        def av_bank(st, hp):
            return PS[:, 512 * (4 + 2 * (st["wi"] % 2) + hp):
                      512 * (5 + 2 * (st["wi"] % 2) + hp)]

        for i in range(ns + 2):
            if i < ns:
                st = steps[i]
                pair = st["pair"]
                slot = i % 2
                waits = list(t_pf[pair][0])
                if i == 0:
                    waits.append(("cs", c_geo_qpe))
                if i >= 2:
                    waits.append(("cs", c_exp[i - 2]))

                def f_smm(e, st=st, slot=slot, pair=pair):
                    v0 = st["v0"]
                    ins = None
                    for hp in range(2):
                        r0 = 64 * hp
                        ins = e.matmul(
                            psS[slot][:, 512 * hp + v0: 512 * (hp + 1)],
                            Ksb[pair % 2][r0: r0 + 64,
                                          128 * st["kt"]: 128 * (st["kt"] + 1)],
                            Qsb[pair % 2][r0: r0 + 64,
                                          st["colbase"] + v0: st["colbase"] + 512],
                            start=True, stop=True, tile_position=(r0, 0))
                    ins.then_inc(CT, 1)
                emitw("t", waits, f_smm)
                c_smm[i] = pb.inc("ct")

            j = i - 2
            if j >= 0:
                st = steps[j]
                pair = st["pair"]
                slot = j % 2
                waits = [("cs", c_exp[j])]
                if st["diag"]:
                    waits.append(("cv", c_mask[j]))
                if j == 0:
                    waits.append(("cs", max(tr_state["evac"])))
                if st["first"]:
                    if st["wi"] == 0:
                        waits.extend(t_pf[pair][1])
                        waits.append(("cv", c_ones[pair]))
                    for hp in range(2):
                        pk = (st["pw"] - 2, hp)
                        if pk in c_norm:
                            waits.append(("cv", c_norm[pk]))
                qt0 = st["v0"] // 128

                def f_avmm(e, st=st, slot=slot, qt0=qt0, pair=pair, j=j):
                    kt, q_lo = st["kt"], st["q_lo"]
                    ins = None
                    for hp in range(2):
                        ps = av_bank(st, hp)
                        for qt in range(qt0, 4):
                            start = st["first"] and qt == 0
                            stop = qt == 3 and st["last"]
                            ins = e.matmul(
                                ps[:, 128 * qt: 128 * qt + 65],
                                Pt[j % 3][:, 512 * hp + 128 * qt: 512 * hp + 128 * (qt + 1)],
                                Vsb[pair % 2][:, VSP * kt + 66 * hp:
                                              VSP * kt + 66 * hp + 65],
                                start=start, stop=stop)
                    ins.then_inc(CT, 1)
                emitw("t", waits, f_avmm)
                c_avmm[j] = pb.inc("ct")

            if i < ns:
                st = steps[i]
                slot = i % 2
                pslot3 = i % 3
                waits = [("ct", c_smm[i])]
                if i >= 3:
                    waits.append(("ct", c_avmm[i - 3]))
                    if steps[i - 3]["diag"]:
                        waits.append(("cv", c_mask[i - 3]))

                def f_exp(e, st=st, slot=slot, pslot3=pslot3):
                    v0 = st["v0"]
                    in3 = psS[slot][:].rearrange("p (h w) -> p h w", h=2)[:, :, v0:512]
                    out3 = Pt[pslot3][:].rearrange("p (h w) -> p h w", h=2)[:, :, v0:512]
                    e.activation(out3, in3, AF.Exp, bias=-EXPC).then_inc(CS, 1)
                emitw("s", waits, f_exp)
                c_exp[i] = pb.inc("cs")

                if st["diag"]:
                    def f_mask(e, st=st, pslot3=pslot3):
                        v0 = st["v0"]
                        seg = Pt[pslot3][:].rearrange("p (h w) -> p h w", h=2)[:, :, v0:v0 + 128]
                        msk = stair_sb[:].unsqueeze(1).to_broadcast([128, 2, 128])
                        e.tensor_mul(seg, seg, msk).then_inc(CV, 1)
                    emitw("v", [("cs", c_exp[i])], f_mask)
                    c_mask[i] = pb.inc("cv")

            if j >= 0 and steps[j]["last"]:
                st = steps[j]
                pw, pair, wi = st["pw"], st["pair"], st["wi"]
                pslot = pair % 2
                for hp in range(2):
                    ps = av_bank(st, hp)
                    rcol = 8 * (st["wi"] % 2) + 4 * hp
                    waits = [("ct", c_avmm[j])]
                    pk = (pw - 2, hp)
                    if pk in c_norm:
                        waits.append(("cv", c_norm[pk]))
                    if hp == 0 and pair >= 2:
                        tok = t_avdump.get((pair - 2, wi))
                        if tok:
                            waits.append(tok)

                    def f_norm(e, ps=ps, rcol=rcol):
                        den = ps.rearrange("p (q w) -> p q w", w=128)[:, :, 64:65].squeeze(2)
                        e.reciprocal(recip_sb[:, rcol: rcol + 4], den).then_inc(CV, 1)
                    emitw("v", waits, f_norm)
                    c_rc = pb.inc("cv")

                    def f_nev(e, st=st, ps=ps, pslot=pslot, hp=hp, rcol=rcol):
                        out3 = AVpair[pslot][:].rearrange(
                            "p (tt w) -> p tt w", w=128)[:, 4 * st["wi"]: 4 * st["wi"] + 4,
                                                         64 * hp: 64 * hp + 64]
                        in3 = ps.rearrange("p (q w) -> p q w", w=128)[:, :, 0:64]
                        rec = recip_sb[:, rcol: rcol + 4].unsqueeze(2).to_broadcast([128, 4, 64])
                        e.tensor_tensor(out3, in3, rec, op=ALU.mult).then_inc(CV, 1)
                    emitw("v", [("cv", c_rc)], f_nev)
                    c_norm[(pw, hp)] = pb.inc("cv")

                toks = []
                for tt in range(4 * wi, 4 * wi + 4):
                    w = [("cv", c_norm[(pw, 1)])]
                    if tt == 4 * wi and pair >= 2:
                        w.append(t_avdump.get((pair - 2, wi), None))
                    if tt > 4 * wi:
                        w = ()
                    toks.append(dma(AVst[pslot][:, 128 * tt: 128 * (tt + 1)],
                                    AVpair[pslot][:, 128 * tt: 128 * (tt + 1)],
                                    transpose=True, waits=[x for x in w if x]))
                t_avdump[(pair, wi)] = dma(
                    bass.AP(av_scr.tensor, 128 * NOWN * pair + 512 * wi,
                            [[NOWN, 128], [1, 512]]),
                    AVst[pslot][:, 512 * wi: 512 * (wi + 1)],
                    waits=comp(toks))

                if wi == 3:
                    c_pair_last[pair] = c_avmm[j]
                    nxt = pair + 2
                    if nxt < 8:
                        t_pf[nxt] = prefetch_pair(nxt, [("ct", c_pair_last[pair])])
                        c_ones[nxt] = c_ones01[nxt % 2]
        pb.end_arm()
    pb.end_switch()

    barrier()
    sb_free({"recip_sb", "AVst0", "AVst1", "AVpair0", "AVpair1", "Pt0", "Pt1",
             "Pt2", "Vsb1", "Qsb0", "Qsb1", "Ksb0", "Ksb1"})

    xnew = sbuf("xnew", [128, NT_O * D], F32)
    avt = [sbuf(f"avt{s}", [128, 1024], BF16) for s in range(2)]
    xo2 = [sbuf(f"xo2{s}", [128, D], F32) for s in range(2)]

    psO = [bank(0, 2), bank(2, 2)]
    ht2 = [sbuf(f"ht2{s}", [128, D], BF16) for s in range(2)]
    sq2 = sbuf("sq2", [128, D], F32)
    st2b = sbuf("st2b", [128, 16], F32)
    hz2 = {"tr": [[], []], "trev": [0, 0], "stat_rd": [0, 0], "sq_wr": 0}
    t_xn = {}
    c_o_evac = {}
    c_omm_hist = {}
    pend_tr2 = []

    def emit_tr2(t, slot, c_ht2):
        ptr = PS[:].bitcast(BF16)[:, 1024 * (4 + slot): 1024 * (5 + slot)]
        wl = [t_ident, ("cv", c_ht2)]
        if hz2["trev"][slot]:
            wl.append(("cv", hz2["trev"][slot]))

        def f_tr2(e, slot=slot, ptr=ptr):
            ins = None
            for k in range(8):
                ins = e.transpose(ptr[:, 128 * k: 128 * (k + 1)],
                                  ht2[slot][:, 128 * k: 128 * (k + 1)],
                                  ident_sb[:])
            ins.then_inc(CT, 1)
        emitw("t", wl, f_tr2)
        c_tr2 = pb.inc("ct")
        hz2["tr"][slot] = [("ct", c_tr2)]

        def f_tr2ev(e, t=t, ptr=ptr):
            out3 = h2[:].rearrange("p (k w) -> p k w", k=8)[:, :, 128 * t: 128 * (t + 1)]
            in3 = ptr.rearrange("p (k w) -> p k w", k=8)
            e.tensor_copy(out3, in3).then_inc(CV, 1)
        emitw("v", [("ct", c_tr2)], f_tr2ev)
        hz2["trev"][slot] = pb.inc("cv")
    for t in range(NT_O):
        slot = t % 2
        w = [("cv", c_o_evac[t - 2])] if t >= 2 else ()
        t_xo = dma(xo2[slot][:], xo_h[128 * t: 128 * (t + 1), :], waits=w)
        wa = [("ct", c_omm_hist[t - 2])] if t >= 2 else ()
        t_av = gdma(
            bass.AP(avt[slot], 0, [[1024, 128], [128, 8], [1, 128]]),
            bass.AP(av_scr.tensor, 128 * t, [[NOWN, 128], [128 * NOWN, 8], [1, 128]]),
            waits=wa)

        wl = t_wout + [t_av] + ([("cv", c_o_evac.get(t - 2, 0))] if t >= 2 else [])

        def f_omm(e, t=t, slot=slot):
            ins = None
            for c in range(8):
                for half in range(2):
                    ins = e.matmul(
                        psO[slot][:, 512 * half: 512 * (half + 1)],
                        avt[slot][:, 128 * c: 128 * (c + 1)],
                        wout_bf[:, 1024 * c + 512 * half: 1024 * c + 512 * (half + 1)],
                        start=(c == 0), stop=(c == 7))
            ins.then_inc(CT, 1)
        emitw("t", wl, f_omm)
        c_omm = pb.inc("ct")
        c_omm_hist[t] = c_omm

        def f_oev(e, t=t, slot=slot):
            e.tensor_add(xnew[:, D * t: D * (t + 1)], xo2[slot][:],
                         psO[slot][:, 0:1024]).then_inc(CV, 1)
        emitw("v", [("ct", c_omm), t_xo], f_oev)
        c_o_evac[t] = pb.inc("cv")

        st = st2b[:, 8 * slot: 8 * slot + 8]
        xt = xnew[:, D * t: D * (t + 1)]
        ht = ht2[slot][:]
        t_xn[t] = dma(xn_scr[t], xt, waits=[("cv", c_o_evac[t])])
        emitw("v", [("cv", max(hz2["stat_rd"][slot], c_o_evac[t]))],
              lambda e, xt=xt, st=st: e.reduce_sum(
                  st[:, 0:1], xt, axis=AX.X).then_inc(CV, 1))
        c_sum = pb.inc("cv")
        emitw("s", [("cs", hz2["sq_wr"]), ("cv", c_o_evac[t])],
              lambda e, xt=xt, st=st: e.activation(
                  sq2[:], xt, AF.Square, accum_out=st[:, 1:2]).then_inc(CS, 1))
        c_sq = pb.inc("cs")
        hz2["sq_wr"] = c_sq

        def f_stats(e, st=st, ws=c_sum, wq=c_sq):
            e.wait_ge(CV, ws)
            e.wait_ge(CS, wq)
            e.tensor_scalar(st[:, 2:3], st[:, 0:1], 1.0 / D, None, op0=ALU.mult).then_inc(CV, 1)
            e.wait_ge(CV, ws + 1)
            e.tensor_mul(st[:, 3:4], st[:, 2:3], st[:, 2:3]).then_inc(CV, 1)
            e.wait_ge(CV, ws + 2)
            e.scalar_tensor_tensor(st[:, 4:5], st[:, 1:2], 1.0 / D, st[:, 3:4],
                                   op0=ALU.mult, op1=ALU.subtract).then_inc(CV, 1)
        pb.emit("v", f_stats)
        pb.inc("cv", 2)
        c_var = pb.inc("cv")
        emitw("s", [("cv", c_var)],
              lambda e, st=st: e.activation(st[:, 5:6], st[:, 4:5], AF.Sqrt,
                                            bias=EPS).then_inc(CS, 1))
        c_std = pb.inc("cs")
        emitw("v", [("cs", c_std)],
              lambda e, st=st: e.reciprocal(st[:, 6:7], st[:, 5:6]).then_inc(CV, 1))
        c_r = pb.inc("cv")
        wl = [("cv", c_r)] + (hz2["tr"][slot] or [])
        emitw("v", wl,
              lambda e, st=st, xt=xt, ht=ht: e.tensor_scalar(
                  ht, xt, st[:, 2:3], st[:, 6:7],
                  op0=ALU.subtract, op1=ALU.mult).then_inc(CV, 1))
        c_ht2 = pb.inc("cv")
        hz2["stat_rd"][slot] = c_ht2
        pend_tr2.append((t, slot, c_ht2))
        if len(pend_tr2) >= 2 or t == NT_O - 1:
            for (tt_, slot_, cht_) in pend_tr2 if t == NT_O - 1 else [pend_tr2.pop(0)]:
                emit_tr2(tt_, slot_, cht_)
            if t == NT_O - 1:
                pend_tr2.clear()

    t_tr2 = [("cv", max(hz2["trev"]))]
    barrier()
    sb_free({"st2b", "sq2", "ht20", "ht21", "xo20", "xo21", "avt0", "avt1", "xnew",
             "wout_bf"})

    hid = [sbuf(f"hid{s}", [128, 24 * 1024], BF16) for s in range(2)]
    sg = sbuf("sg", [128, 1024], BF16)
    xnt = sbuf("xnt", [128, D], F32)
    yst = sbuf("yst", [128, D], F32)

    psG = [bank(0, 2), bank(4, 2)]
    psU = [bank(2, 2), bank(6, 2)]

    c_gu = {}
    c_sg = {}
    c_hid = {}
    for hc in range(24):
        slot = hc % NWS
        wcast = [("ct", c_gu[(hc - NWS, 1)])] if hc >= NWS else []
        wgu_in = bass.AP(wgu_h.tensor, 1024 * 256 * hc,
                         [[256, 128], [128 * 256, 8], [1, 256]])
        t_cgu = gdma(wguT[slot][:].rearrange("p (k w) -> p k w", k=8),
                     wgu_in, waits=wcast)

        for sbi in range(2):
            wl = [t_cgu] + t_tr2
            if (hc - 1, sbi) in c_hid:
                wl.append(("cv", c_hid[(hc - 1, sbi)]))

            def f_gu(e, sbi=sbi, slot=slot):
                ins = None
                for half in range(2):
                    for k in range(8):
                        ins = e.matmul(
                            psG[sbi][:, 512 * half: 512 * (half + 1)],
                            wguT[slot][:, 256 * k: 256 * k + 128],
                            h2[:, 2048 * k + 1024 * sbi + 512 * half:
                               2048 * k + 1024 * sbi + 512 * (half + 1)],
                            start=(k == 0), stop=(k == 7))
                for half in range(2):
                    for k in range(8):
                        ins = e.matmul(
                            psU[sbi][:, 512 * half: 512 * (half + 1)],
                            wguT[slot][:, 256 * k + 128: 256 * (k + 1)],
                            h2[:, 2048 * k + 1024 * sbi + 512 * half:
                               2048 * k + 1024 * sbi + 512 * (half + 1)],
                            start=(k == 0), stop=(k == 7))
                ins.then_inc(CT, 1)
            emitw("t", wl, f_gu)
            c_gu[(hc, sbi)] = pb.inc("ct")

            wl = [("ct", c_gu[(hc, sbi)])]
            if (hc - 1, sbi) in c_hid:
                wl.append(("cv", c_hid[(hc - 1, sbi)]))
            prev = (hc, 0) if sbi == 1 else (hc - 1, 1)
            if prev in c_hid:
                wl.append(("cv", c_hid[prev]))

            def f_silu(e, sbi=sbi):
                e.activation(sg[:], psG[sbi][:, 0:1024], AF.Silu,
                             scale=Q127).then_inc(CS, 1)
            emitw("s", wl, f_silu)
            c_sg[(hc, sbi)] = pb.inc("cs")

            def f_hf(e, hc=hc, sbi=sbi):
                e.scalar_tensor_tensor(hid[sbi][:, 1024 * hc: 1024 * (hc + 1)],
                                       psU[sbi][:, 0:1024], Q127, sg[:],
                                       op0=ALU.mult, op1=ALU.mult).then_inc(CV, 1)
            emitw("v", [("cs", c_sg[(hc, sbi)])], f_hf)
            c_hid[(hc, sbi)] = pb.inc("cv")

    psD = [bank(0, 2), bank(2, 2)]
    t_out_last = None
    c_dn_ev = {}
    t_y_hist = {}
    for t in range(NT_O):
        sbi, tt = divmod(t, 8)
        slot = t % 2
        wxn = [t_xn[t]]
        if t >= 1:
            wxn.append(("cv", c_dn_ev[t - 1]))
        t_xnt = dma(xnt[:], xn_scr[t], waits=wxn)

        wl = [("cv", c_hid[(23, sbi)]), ("cv", c_hid[(23, 0)])] + t_wdT
        if t >= 2:
            wl.append(("cv", c_dn_ev[t - 2]))

        def f_dn(e, sbi=sbi, tt=tt, slot=slot):
            ins = None
            for hc in range(24):
                for half in range(2):
                    ins = e.matmul(
                        psD[slot][:, 512 * half: 512 * (half + 1)],
                        hid[sbi][:, 1024 * hc + 128 * tt: 1024 * hc + 128 * (tt + 1)],
                        wdT[:, 1024 * hc + 512 * half: 1024 * hc + 512 * (half + 1)],
                        start=(hc == 0), stop=(hc == 23))
            ins.then_inc(CT, 1)
        emitw("t", wl, f_dn)
        c_dmm = pb.inc("ct")

        wl = [("ct", c_dmm), t_xnt]
        if t_y_hist.get(t - 1):
            wl.append(t_y_hist[t - 1])

        def f_dev(e, slot=slot):
            e.scalar_tensor_tensor(yst[:], psD[slot][:, 0:1024], Q127,
                                   xnt[:],
                                   op0=ALU.mult, op1=ALU.add).then_inc(CV, 1)
        emitw("v", wl, f_dev)
        c_dn_ev[t] = pb.inc("cv")
        t_out_last = dma(y_h[128 * t: 128 * (t + 1), :], yst[:],
                         waits=[("cv", c_dn_ev[t])])
        t_y_hist[t] = t_out_last

    emitw("y", [t_out_last], lambda e: None)

    def replay(ekey, eng):
        for it in pb.items[ekey]:
            if isinstance(it, SwitchItem):
                if not (it.arms[0][ekey] or it.arms[1][ekey]):
                    continue
                par = eng.snap(eng.partition_id() % 2)
                for a in nc.Switch(engines=[eng], index=[par], n=2):
                    for f in it.arms[a][ekey]:
                        f(eng)
            else:
                it(eng)

    with nc.Block() as blk:
        @blk.tensor
        def _(e):
            replay("t", e)

        @blk.scalar
        def _(e):
            replay("s", e)

        @blk.vector
        def _(e):
            replay("v", e)

        @blk.sync
        def _(e):
            replay("y", e)

        @blk.gpsimd
        def _(e):
            replay("g", e)

    return nc


_CACHED = {}


def _get_nc():
    if "nc" not in _CACHED:
        _CACHED["nc"] = build_program()
    return _CACHED["nc"]


def _own_idx(p):
    return np.r_[0:1024, 3072:4096] if p == 0 else np.r_[1024:3072]


def make_in_maps(x, wqkv, wout, hs, hd, wg, wu, wd):
    dgeo = np.zeros((128, 16), np.float32)
    hsq = np.zeros((16, 128), np.float32)
    dq = np.zeros((16, 128), np.float32)
    for h in range(16):
        m, j = h // 2, h % 2
        dgeo[64 * j + 8 * m: 64 * j + 8 * m + 8, h] = hd[h]
        hsq[h, 8 * h: 8 * h + 8] = hs[h]
        dq[h, 8 * h: 8 * h + 8] = hd[h]
    cst = np.ascontiguousarray(
        np.tile(np.asarray(CONST_VALS, np.float32)[None, :], (128, 1)))
    kk, jj = np.meshgrid(np.arange(128), np.arange(128), indexing="ij")
    stair = np.ascontiguousarray((jj >= kk).astype(BF16NP))
    ident = np.ascontiguousarray(np.eye(128).astype(BF16NP))
    wgu = np.ascontiguousarray(np.concatenate(
        [wg.T.reshape(D, 24, 128).transpose(1, 0, 2),
         wu.T.reshape(D, 24, 128).transpose(1, 0, 2)], axis=2))
    wdt = np.ascontiguousarray(wd.T)
    in_maps = []
    for c in range(8):
        b, p = c // 2, c % 2
        in_maps.append({
            "xf": x[b],
            "xo": np.ascontiguousarray(x[b][_own_idx(p)]),
            "wqkv": wqkv, "wout": wout,
            "dgeo": dgeo, "hsq": hsq, "dq": dq,
            "wgu": wgu, "wdt": wdt,
            "cst": cst, "stair": stair, "ident": ident,
        })
    return in_maps


def kernel(L0_attn_x, L0_attn_w_qkv, L0_attn_w_out, L0_attn_head_scales,
           L0_attn_head_dirs, L0_ffn_w_gate, L0_ffn_w_up, L0_ffn_w_down):
    x = np.ascontiguousarray(np.asarray(L0_attn_x, dtype=np.float32))
    wqkv = np.ascontiguousarray(np.asarray(L0_attn_w_qkv, dtype=np.float32))
    wout = np.ascontiguousarray(np.asarray(L0_attn_w_out, dtype=np.float32))
    hs = np.ascontiguousarray(np.asarray(L0_attn_head_scales, dtype=np.float32))
    hd = np.ascontiguousarray(np.asarray(L0_attn_head_dirs, dtype=np.float32))
    wg = np.ascontiguousarray(np.asarray(L0_ffn_w_gate, dtype=np.int32))
    wu = np.ascontiguousarray(np.asarray(L0_ffn_w_up, dtype=np.int32))
    wd = np.ascontiguousarray(np.asarray(L0_ffn_w_down, dtype=np.int32))

    nc = _get_nc()
    in_maps = make_in_maps(x, wqkv, wout, hs, hd, wg, wu, wd)
    res = run_bass_kernel_spmd(nc, in_maps, core_ids=list(range(8)))
    out = np.empty((B, T, D), np.float32)
    for c in range(8):
        b, p = c // 2, c % 2
        out[b][_own_idx(p)] = np.asarray(res.results[c]["y"], dtype=np.float32)
    return out
